# version 15
# baseline (speedup 1.0000x reference)
import numpy as np
from contextlib import ExitStack

import concourse.bass as bass
import concourse.mybir as mybir
from concourse.bass_utils import run_bass_kernel_spmd

F32 = mybir.dt.float32
BF16 = mybir.dt.bfloat16
U32 = mybir.dt.uint32
AF = mybir.ActivationFunctionType
ALU = mybir.AluOpType
AX = mybir.AxisListType.X

B, T, C, H, D = 4, 8192, 1024, 16, 64
NCORE = 8
TOK = T * B // NCORE
NT = TOK // 128
PFX = 64
NSEQ = float(T)
LN_EPS = 1e-5
SOFT_SHIFT = 2.5
GL = 512
NG = TOK // GL

DEBUG_OUTPUTS = False
STAGE = 4
L1SUB = 5
ATTNPARTS = 4



class _Stream:
    def __init__(self, name):
        self.name = name
        self.sem = None
        self.count = 0
        self.waited = {}
        self.items = []


class Tok:
    __slots__ = ("stream", "value")
    def __init__(self, stream, value):
        self.stream = stream
        self.value = value


class Buf:
    __slots__ = ("name", "writer", "readers")
    def __init__(self, name=""):
        self.name = name
        self.writer = None
        self.readers = {}


class Chan:
    def __init__(self, name, stream):
        self.name = name
        self.stream = stream
        self.sem = None
        self.count = 0


_SEEN_INC = {}


class Bld:
    ENGINES = ("tensor", "vector", "scalar", "gpsimd", "sync")

    def __init__(self, nc):
        self.nc = nc
        self.streams = {e: _Stream(e) for e in self.ENGINES}
        self.chans = []

    def chan(self, name, stream_name):
        c = Chan(name, self.streams[stream_name])
        self.chans.append(c)
        return c

    def _dep_toks(self, reads, writes):
        toks = []
        for b in reads:
            if b.writer is not None:
                toks.append(b.writer)
        for b in writes:
            if b.writer is not None:
                toks.append(b.writer)
            for s, v in b.readers.items():
                toks.append(Tok(s, v))
        return toks

    def _emit_waits(self, stream, toks):
        need = {}
        for t in toks:
            src = t.stream
            if src is stream and stream.name == "tensor":
                continue
            if t.value > need.get(src, 0):
                need[src] = t.value
        for src, v in need.items():
            if v > stream.waited.get(src, 0):
                stream.waited[src] = v
                stream.items.append(("w", src, v))

    def _track(self, src, count, reads, writes):
        tok = Tok(src, count)
        for b in reads:
            if count > b.readers.get(src, 0):
                b.readers[src] = count
        for b in writes:
            b.writer = tok
            b.readers = {}
        return tok

    def on(self, stream_name, reads, writes, fn):
        stream = self.streams[stream_name]
        self._emit_waits(stream, self._dep_toks(reads, writes))

        import traceback
        tb = traceback.extract_stack()[-3:-1]
        cnt = stream.count + 1

        def item(eng, fn=fn, stream=stream, tb=tb, cnt=cnt):
            last = fn(eng)
            assert last is not None, "step fn must return its last instruction"
            key = id(last)
            if key in _SEEN_INC:
                raise RuntimeError(
                    f"step #{cnt} on {stream.name} returned an instruction already "
                    f"inc'd by {_SEEN_INC[key]}; this step from {tb}")
            _SEEN_INC[key] = (stream.name, cnt, str(tb), last)
            last.then_inc(stream.sem, 1)
        stream.items.append(("f", item))
        stream.count += 1
        return self._track(stream, stream.count, reads, writes)

    def dma(self, chan, out, in_, reads, writes, **kw):
        stream = chan.stream
        if chan.count > 0:
            self.wait_chan(stream.name, chan)
        self._emit_waits(stream, self._dep_toks(reads, writes))

        def item(eng, out=out, in_=in_, chan=chan, kw=kw):
            eng.dma_start(out=out, in_=in_, **kw).then_inc(chan.sem, 16)
        stream.items.append(("f", item))
        chan.count += 16
        return self._track(chan, chan.count, reads, writes)

    def wait_chan(self, stream_name, chan, value=None):
        stream = self.streams[stream_name]
        v = chan.count if value is None else value
        if v > stream.waited.get(chan, 0):
            stream.waited[chan] = v
            stream.items.append(("w", chan, v))

    def barrier(self, skip_streams=(), skip_chans=()):
        snap = [(s, s.count) for s in self.streams.values()
                if s.count > 0 and s.name not in skip_streams]
        snapc = [(c, c.count) for c in self.chans
                 if c.count > 0 and c not in skip_chans]
        for s in self.streams.values():
            for src, v in snap:
                if src is s:
                    continue
                if v > s.waited.get(src, 0):
                    s.waited[src] = v
                    s.items.append(("w", src, v))
            for ch, v in snapc:
                if v > s.waited.get(ch, 0):
                    s.waited[ch] = v
                    s.items.append(("w", ch, v))

    def alloc_sems(self, es):
        for s in self.streams.values():
            s.sem = es.enter_context(self.nc.semaphore(f"sem_{s.name}"))
        for c in self.chans:
            c.sem = es.enter_context(self.nc.semaphore(f"ch_{c.name}"))

    def emit(self, block):
        for name in self.ENGINES:
            s = self.streams[name]
            items = s.items

            def run(eng, items=items):
                for it in items:
                    if it[0] == "w":
                        eng.wait_ge(it[1].sem, it[2])
                    else:
                        it[1](eng)
            getattr(block, name)(run)



def _finish_stub(nc, bld, es, y_out, xbuf, b_xbuf, ch_st, ch_x, x_in, src_scr=None):
    for t in range(NT):
        bi = t % 2
        xb, bxb = xbuf[bi], b_xbuf[bi]
        src = (src_scr if src_scr is not None else x_in)[t * 128:(t + 1) * 128, :]
        bld.dma(ch_x[bi], xb[:, :], src, [], [bxb])
        bld.dma(ch_st[bi], y_out[t * 128:(t + 1) * 128, :], xb[:, :], [bxb],
                [Buf(f"ystub_{t}")])
    for chn in ch_st:
        bld.wait_chan("gpsimd", chn)
    bld.alloc_sems(es)
    block = es.enter_context(nc.Block())
    bld.emit(block)
    return nc, {}

def build_program(debug=False, stage=4, l1sub=5, attnparts=4):
    nc = bass.Bass("TRN2", target_bir_lowering=False)

    inp = {}
    def ein(name, shape, dtype=F32):
        inp[name] = nc.dram_tensor(name, list(shape), dtype, kind="ExternalInput").ap()
        return inp[name]

    x_in = ein("x", (TOK, C))
    xp_in = ein("xp", (PFX, C))
    for nm in ("q", "k1", "k2", "v1", "v2"):
        ein(f"W{nm}", (C, C)); ein(f"b{nm}", (C,))
    for i in range(1, 5):
        ein(f"g{i}", (H,))
    ein("ln_g", (C,)); ein("ln_b", (C,))
    ein("Wo0", (C, C)); ein("bo0", (C,))
    ein("Wf", (C, C)); ein("bf", (C,))
    ein("lora_down", (4, 100)); ein("lora_up", (50, 4))
    ein("down_aux", (100, C)); ein("up_aux", (C, 50))
    ein("W1q", (C, C)); ein("W1k", (C, C)); ein("W1v", (C, C)); ein("W1o", (C, C))
    ein("Wg", (C, C)); ein("bg", (C,))
    ein("We", (8, C, C)); ein("be", (8, C))
    ein("ident_bf", (128, 128), BF16)
    ein("ident_f32", (128, 128), F32)
    ein("ones_bf", (128, 128), BF16)
    ein("ones_f32", (128, 128), F32)
    ein("iota8", (128, 8), F32)
    ein("kvec", (128, 4), F32)

    y_out = nc.dram_tensor("y", [TOK, C], F32, kind="ExternalOutput").ap()
    dbg = {}
    if debug:
        def dout(name, shape, dtype=F32):
            dbg[name] = nc.dram_tensor(name, list(shape), dtype, kind="ExternalOutput").ap()
        dout("d_x1", (TOK, C))
        dout("d_S1", (128, 512))
        dout("d_M", (128, 512))

    wt_names = ["Wq", "Wk1", "Wk2", "Wv1", "Wv2", "Wo0", "Wf",
                "W1q", "W1k", "W1v", "W1o", "Wg"] + [f"We{e}" for e in range(8)]
    wtbf = {n: nc.dram_tensor(f"wtbf_{n}", [C, C], BF16, kind="Internal").ap()
            for n in wt_names}
    dT_scr = nc.dram_tensor("dT_scr", [C, C], BF16, kind="Internal").ap()
    wtbfT = {n: nc.dram_tensor(f"wtbfT_{n}", [128, 8 * C], BF16, kind="Internal").ap()
             for n in ("W1q", "W1k", "W1o")}
    x1_scr = nc.dram_tensor("x1_scr", [TOK, C], F32, kind="Internal").ap()
    xnb_scr = nc.dram_tensor("xnb_scr", [TOK, C], BF16, kind="Internal").ap()
    cc_in_t = nc.dram_tensor("cc_in", [260, 512], F32, kind="Internal")
    cc_out_t = nc.dram_tensor("cc_out", [260, 512], F32, kind="Internal")
    cc_in = cc_in_t.ap()
    cc_out = cc_out_t.ap()

    bld = Bld(nc)
    es = ExitStack()
    with es:
        ch_w = [bld.chan(f"w{i}", "sync") for i in range(2)]
        ch_wc = bld.chan("wc", "gpsimd")
        ch_x = [bld.chan(f"x{i}", "sync") for i in range(2)]
        ch_misc = bld.chan("misc", "gpsimd")
        ch_miscf = bld.chan("miscf", "sync")
        ch_st = [bld.chan(f"st{i}", "gpsimd") for i in range(2)]
        ch_cc = bld.chan("cc", "sync")

        def sbT(stack, name, shape, dt=F32):
            return stack.enter_context(nc.sbuf_tensor(name, list(shape), dt))

        pp = [es.enter_context(nc.psum_tensor(f"p_pp{i}", [128, 512], F32)) for i in range(2)]
        b_pp = [Buf(f"pp{i}") for i in range(2)]
        pS1 = es.enter_context(nc.psum_tensor("p_S1", [128, 512], F32)); b_pS1 = Buf("pS1")
        pS2 = es.enter_context(nc.psum_tensor("p_S2", [128, 512], F32)); b_pS2 = Buf("pS2")
        p8 = es.enter_context(nc.psum_tensor("p_p8", [128, 512], F32)); b_ptr = Buf("p8")
        ptr = p8[0:128, 0:512].bitcast(BF16)
        prs1 = es.enter_context(nc.psum_tensor("p_rs1", [128, 512], F32)); b_prs1 = Buf("prs1")
        prs2 = es.enter_context(nc.psum_tensor("p_rs2", [128, 512], F32)); b_prs2 = Buf("prs2")
        ptrf = es.enter_context(nc.psum_tensor("p_trf", [128, 512], F32)); b_ptrf = Buf("ptrf")

        ident_bf = sbT(es, "s_ident_bf", (128, 128), BF16); b_ident_bf = Buf("ident_bf")
        ident_f = sbT(es, "s_ident_f", (128, 128), F32); b_ident_f = Buf("ident_f")
        ones_bf = sbT(es, "s_ones_bf", (128, 128), BF16); b_ones_bf = Buf("ones_bf")
        ones_f = sbT(es, "s_ones_f", (128, 128), F32); b_ones_f = Buf("ones_f")
        iota8 = sbT(es, "s_iota8", (128, 8), F32); b_iota8 = Buf("iota8")
        kvec = sbT(es, "s_kvec", (128, 4), F32); b_kvec = Buf("kvec")

        NWS = 4
        wslot = [sbT(es, f"s_wslot{i}", (128, 8 * C), BF16) for i in range(NWS)]
        b_wslot = [Buf(f"wslot{i}") for i in range(NWS)]

        brow = sbT(es, "s_brow", (128, 4 * C), BF16); b_brow = Buf("brow")
        BRP = {}
        _brslots = [(p, c) for c in range(4) for p in (0, 32, 64)]

        def brow_ap(name, lo=0, ln=C):
            p, c = BRP[name]
            return brow[p:p + 1, c * C + lo:c * C + lo + ln]

        bcol = sbT(es, "s_bcol", (128, 32), F32); b_bcol = Buf("bcol")
        xbuf = [sbT(es, f"s_xbuf{i}", (128, C), F32) for i in range(2)]
        b_xbuf = [Buf(f"xbuf{i}") for i in range(2)]
        xnbf = [sbT(es, f"s_xnbf{i}", (128, C), BF16) for i in range(2)]
        b_xnbf = [Buf(f"xnbf{i}") for i in range(2)]
        lnst = sbT(es, "s_lnst", (128, 16), F32); b_lnst = Buf("lnst")
        lnsq = sbT(es, "s_lnsq", (128, 4), F32)
        sqscr = sbT(es, "s_sqscr", (128, C), F32); b_sqscr = Buf("sqscr")
        x1p = sbT(es, "s_x1p", (64, C), F32); b_x1p = Buf("x1p")

        bld.dma(ch_miscf, ident_f[:], inp["ident_f32"], [], [b_ident_f])
        bld.dma(ch_miscf, ones_f[:], inp["ones_f32"], [], [b_ones_f])
        bld.dma(ch_miscf, iota8[:], inp["iota8"], [], [b_iota8])
        bld.dma(ch_miscf, kvec[:], inp["kvec"], [], [b_kvec])
        bld.dma(ch_misc, ident_bf[:], inp["ident_bf"], [], [b_ident_bf])
        bld.dma(ch_misc, ones_bf[:], inp["ones_bf"], [], [b_ones_bf])

        def load_brow(name, src=None):
            BRP[name] = _brslots[len(BRP)]
            src = src if src is not None else inp[name].unsqueeze(0)
            p, c = BRP[name]
            bld.dma(ch_misc, brow[p:p + 1, c * C:(c + 1) * C], src, [], [b_brow])
        for nm in ("bq", "bk1", "bk2", "bv1", "bv2", "bg"):
            load_brow(nm)
        BRP["p2bias"] = _brslots[len(BRP)]
        assert BRP["p2bias"][0] == 0 or True
        if BRP["p2bias"][0] != 0:
            for s in _brslots[len(BRP):]:
                if s[0] == 0:
                    BRP["p2bias"] = s
                    break

        b_wtbf = {n: Buf(f"wtbf_{n}") for n in wt_names}
        wsrc = {f"We{e}": inp["We"][e] for e in range(8)}
        for n in ("Wv1", "Wk1", "Wv2", "Wk2"):
            bld.dma(ch_wc, wtbf[n], wsrc.get(n, inp.get(n)), [], [b_wtbf[n]])

        b_wtbfT = {k: Buf(f"wtbfT_{k}") for k in wtbfT}
        _wtbfT_saved = set()

        def load_wT(slotid, name, chi=None):
            ch = ch_w[(slotid if chi is None else chi) % 2]
            if name in wtbfT and name in _wtbfT_saved:
                bld.dma(ch, wslot[slotid][:, :], wtbfT[name],
                        [b_wtbfT[name]], [b_wslot[slotid]])
                return
            for ci in range(8):
                bld.dma(ch, wslot[slotid][:, ci * C:(ci + 1) * C],
                        wtbf[name][0:C, ci * 128:(ci + 1) * 128],
                        [b_wtbf[name]], [b_wslot[slotid]], transpose=True)
            if name in wtbfT:
                bld.dma(ch, wtbfT[name], wslot[slotid][:, :],
                        [b_wslot[slotid]], [b_wtbfT[name]])
                _wtbfT_saved.add(name)

        def load_wN(slotid, name, chi=None):
            ch = ch_w[(slotid if chi is None else chi) % 2]
            src = wtbf[name].rearrange("(r p) c -> p r c", p=128)
            bld.dma(ch, wslot[slotid][:, :].rearrange("p (r c) -> p r c", c=C), src,
                    [b_wtbf[name]], [b_wslot[slotid]])

        es_pro = ExitStack()
        ldsb = sbT(es_pro, "s_ldsb", (128, 128), F32); b_ldsb = Buf("ldsb")
        lusb = sbT(es_pro, "s_lusb", (128, 8), F32); b_lusb = Buf("lusb")
        luT_sb = sbT(es_pro, "s_luT", (128, 64), F32); b_luT = Buf("luT")
        ldT_sb = sbT(es_pro, "s_ldT", (128, 8), F32); b_ldT = Buf("ldT")
        dasb = sbT(es_pro, "s_dasb", (128, 1024), F32); b_dasb = Buf("dasb")
        uasb = sbT(es_pro, "s_uasb", (128, 400), F32); b_uasb = Buf("uasb")
        P1sb = sbT(es_pro, "s_P1sb", (128, 1024), F32); b_P1sb = Buf("P1sb")
        P2sb = sbT(es_pro, "s_P2sb", (128, 1024), F32); b_P2sb = Buf("P2sb")
        uaT = sbT(es_pro, "s_uaT", (128, 1024), F32); b_uaT = Buf("uaT")
        dTt = [sbT(es_pro, f"s_dTt{i}", (128, 1024), BF16) for i in range(2)]
        b_dTt = [Buf(f"dTt{i}") for i in range(2)]

        bld.dma(ch_miscf, ldsb[0:4, 0:100], inp["lora_down"], [], [b_ldsb])
        bld.dma(ch_miscf, lusb[0:50, 0:4], inp["lora_up"], [], [b_lusb])
        bld.dma(ch_miscf, dasb[0:100, :], inp["down_aux"], [], [b_dasb])
        bld.dma(ch_miscf, uasb[:, :].rearrange("p (r k) -> p r k", k=50),
                inp["up_aux"].rearrange("(r p) k -> p r k", p=128),
                [], [b_uasb])

        psm, b_psm = pp[0], b_pp[0]
        bld.on("tensor", [b_ldsb, b_ident_f], [b_psm],
               lambda e: e.transpose(out=psm[0:100, 0:4], in_=ldsb[0:4, 0:100],
                                     identity=ident_f[0:4, 0:4]))
        bld.on("vector", [b_psm], [b_ldT],
               lambda e: e.tensor_copy(out=ldT_sb[0:100, 0:4], in_=psm[0:100, 0:4]))
        for half in range(2):
            psmH, b_psmH = (pp[half], b_pp[half])
            bld.on("tensor", [b_ldT, b_dasb], [b_psmH],
                   lambda e, half=half, psmH=psmH: e.matmul(
                       out=psmH[0:4, 0:512], lhsT=ldT_sb[0:100, 0:4],
                       rhs=dasb[0:100, half * 512:(half + 1) * 512],
                       start=True, stop=True))
            bld.on("vector", [b_psmH], [b_P1sb],
                   lambda e, half=half, psmH=psmH: e.tensor_copy(
                       out=P1sb[0:4, half * 512:(half + 1) * 512], in_=psmH[0:4, 0:512]))
        bld.on("tensor", [b_lusb, b_ident_f], [b_psm],
               lambda e: e.transpose(out=psm[0:4, 0:50], in_=lusb[0:50, 0:4],
                                     identity=ident_f[0:50, 0:50]))
        bld.on("vector", [b_psm], [b_luT],
               lambda e: e.tensor_copy(out=luT_sb[0:4, 0:50], in_=psm[0:4, 0:50]))
        for half in range(2):
            psmH, b_psmH = (pp[half], b_pp[half])
            bld.on("tensor", [b_luT, b_P1sb], [b_psmH],
                   lambda e, half=half, psmH=psmH: e.matmul(
                       out=psmH[0:50, 0:512], lhsT=luT_sb[0:4, 0:50],
                       rhs=P1sb[0:4, half * 512:(half + 1) * 512],
                       start=True, stop=True))
            bld.on("vector", [b_psmH], [b_P2sb],
                   lambda e, half=half, psmH=psmH: e.tensor_copy(
                       out=P2sb[0:50, half * 512:(half + 1) * 512], in_=psmH[0:50, 0:512]))
        for r in range(8):
            psmH, b_psmH = (pp[r % 2], b_pp[r % 2])
            bld.on("tensor", [b_uasb, b_ident_f], [b_psmH],
                   lambda e, r=r, psmH=psmH: e.transpose(out=psmH[0:50, 0:128],
                                                         in_=uasb[:, r * 50:(r + 1) * 50],
                                                         identity=ident_f[:, :]))
            bld.on("vector", [b_psmH], [b_uaT],
                   lambda e, r=r, psmH=psmH: e.tensor_copy(out=uaT[0:50, r * 128:(r + 1) * 128],
                                                           in_=psmH[0:50, 0:128]))
        b_dTscr = Buf("dTscr")
        for ci in range(8):
            bsel = ci % 2
            psmX, b_psmX = pp[bsel], b_pp[bsel]
            for half in range(2):
                bld.on("tensor", [b_P2sb, b_uaT], [b_psmX],
                       lambda e, ci=ci, half=half, psmX=psmX: e.matmul(
                           out=psmX[0:128, 0:512],
                           lhsT=P2sb[0:50, ci * 128:(ci + 1) * 128],
                           rhs=uaT[0:50, half * 512:(half + 1) * 512],
                           start=True, stop=True))
                bld.on("vector", [b_psmX], [b_dTt[bsel]],
                       lambda e, half=half, psmX=psmX, bsel=bsel: e.tensor_copy(
                           out=dTt[bsel][:, half * 512:(half + 1) * 512],
                           in_=psmX[0:128, 0:512]))
            bld.dma(ch_cc, dT_scr[ci * 128:(ci + 1) * 128, :], dTt[bsel][:, :],
                    [b_dTt[bsel]], [b_dTscr])
        bld.barrier(skip_streams=("gpsimd",), skip_chans=(ch_wc, ch_misc))
        es_pro.close()
        if stage == 0:
            return _finish_stub(nc, bld, es, y_out, xbuf, b_xbuf, ch_st, ch_x, x_in)
        for n in ["Wq", "Wo0", "Wf", "W1q", "W1k", "W1v", "Wg", "W1o"] + \
                [f"We{e}" for e in range(8)]:
            bld.dma(ch_wc, wtbf[n], wsrc.get(n, inp.get(n)), [], [b_wtbf[n]])

        es_l0 = ExitStack()
        Mbf = sbT(es_l0, "s_Mbf", (128, 512), BF16); b_Mbf = Buf("Mbf")
        y2bf = sbT(es_l0, "s_y2bf", (128, 512), BF16); b_y2bf = Buf("y2bf")

        es_a = ExitStack()
        S1sb = sbT(es_a, "s_S1sb", (128, 512), F32); b_S1sb = Buf("S1sb")
        S2sb = sbT(es_a, "s_S2sb", (128, 512), F32); b_S2sb = Buf("S2sb")
        rowacc = sbT(es_a, "s_rowacc", (128, 2048), F32); b_rowacc = Buf("rowacc")
        Gsb = sbT(es_a, "s_Gsb", (128, 8 * C), BF16); b_Gsb = Buf("Gsb")
        Tsb = sbT(es_a, "s_Tsb", (128, 8 * C), BF16); b_Tsb = Buf("Tsb")
        rows2 = sbT(es_a, "s_rows2", (128, 10 * C), BF16); b_rows2 = Buf("rows2")
        xnsr = sbT(es_a, "s_xnsr", (128, 512), F32); b_xnsr = Buf("xnsr")
        xnsrb = sbT(es_a, "s_xnsrb", (1, 1024), BF16); b_xnsrb = Buf("xnsrb")
        xnscol = sbT(es_a, "s_xnscol", (128, 16), BF16); b_xnscol = Buf("xnscol")
        b_xnscr = Buf("xnscr")
        vsum = sbT(es_a, "s_vsum", (128, 2048), F32); b_vsum = Buf("vsum")
        p0pool = sbT(es_a, "s_p0pool", (128, 2048), F32); b_p0 = Buf("p0pool")
        srow_bf = sbT(es_a, "s_srow_bf", (128, 3072), BF16); b_srow_bf = Buf("srow_bf")
        gsb = sbT(es_a, "s_gsb", (128, 64), F32); b_gsb = Buf("gsb")
        frow = sbT(es_a, "s_frow", (128, 1024), F32); b_frow = Buf("frow")
        Mtmp = sbT(es_a, "s_Mtmp", (128, 512), F32); b_Mtmp = Buf("Mtmp")
        lnbbf = sbT(es_a, "s_lnbbf", (128, 8), BF16); b_lnbbf = Buf("lnbbf")

        b_lnA = Buf("lnA")

        def ln_tile(x_src, width, xb, b_xb, xnb, b_xnb, chan):
            bld.dma(chan, xb[0:width, :], x_src, [], [b_xb])
            bld.on("scalar", [b_xb], [b_sqscr, b_lnA],
                   lambda e, xb=xb, w=width: e.activation(
                       out=sqscr[0:w, :], in_=xb[0:w, :], func=AF.Square,
                       accum_out=lnsq[0:w, 0:1]))
            bld.on("vector", [b_xb], [b_lnst],
                   lambda e, xb=xb, w=width: e.reduce_sum(
                       out=lnst[0:w, 0:1], in_=xb[0:w, :], axis=AX))
            bld.on("vector", [b_lnst], [b_lnst],
                   lambda e, w=width: e.tensor_single_scalar(
                       out=lnst[0:w, 2:3], in_=lnst[0:w, 0:1], scalar=1.0 / C, op=ALU.mult))
            bld.on("vector", [b_lnst], [b_lnst],
                   lambda e, w=width: e.tensor_scalar(
                       out=lnst[0:w, 3:4], in0=lnst[0:w, 0:1],
                       scalar1=lnst[0:w, 0:1], scalar2=1.0 / (C * C),
                       op0=ALU.mult, op1=ALU.mult))
            bld.on("vector", [b_lnst, b_lnA], [b_lnst],
                   lambda e, w=width: e.scalar_tensor_tensor(
                       out=lnst[0:w, 4:5], in0=lnsq[0:w, 0:1], scalar=1.0 / C,
                       in1=lnst[0:w, 3:4], op0=ALU.mult, op1=ALU.subtract))
            bld.on("scalar", [b_lnst], [b_lnA],
                   lambda e, w=width: e.activation(
                       out=lnst[0:w, 5:6], in_=lnst[0:w, 4:5], func=AF.Ln, bias=kvec[0:w, 0:1]))
            bld.on("scalar", [b_lnA], [b_lnA],
                   lambda e, w=width: e.activation(
                       out=lnst[0:w, 6:7], in_=lnst[0:w, 5:6], func=AF.Exp, scale=-0.5))
            bld.on("vector", [b_xb, b_lnst, b_lnA], [b_xnb],
                   lambda e, xb=xb, xnb=xnb, w=width: e.tensor_scalar(
                       out=xnb[0:w, :], in0=xb[0:w, :],
                       scalar1=lnst[0:w, 2:3], scalar2=lnst[0:w, 6:7],
                       op0=ALU.subtract, op1=ALU.mult))

        def transpose_tile_to_fm(xnb, b_xnb, dst_ap, b_dst, width=128):
            def mmfn(e, xnb=xnb, w=width):
                last = None
                for ci in range(8):
                    last = e.transpose(out=ptr[0:128, ci * 128:ci * 128 + w],
                                       in_=xnb[0:w, ci * 128:(ci + 1) * 128],
                                       identity=ident_bf[0:w, 0:w])
                return last
            bld.on("tensor", [b_xnb, b_ident_bf], [b_ptr], mmfn)

            def cpfn(e, dst_ap=dst_ap, w=width):
                src = ptr[0:128, :].rearrange("p (ci t) -> p ci t", ci=8)[:, :, 0:w]
                return e.tensor_copy(out=dst_ap, in_=src)
            bld.on("vector", [b_ptr], [b_dst], cpfn)

        GB_A = [pp[0], pp[1], pS1, pS2, p8, prs1, prs2, ptrf]
        b_GBA = [b_pp[0], b_pp[1], b_pS1, b_pS2, b_ptr, b_prs1, b_prs2, b_ptrf]

        ln_tile(x_in[0:128, :], 128, xbuf[0], b_xbuf[0], xnbf[0], b_xnbf[0], ch_x[0])
        for t in range(NT):
            bi = t % 2
            xnb, bxnb = xnbf[bi], b_xnbf[bi]
            if t + 1 < NT:
                bj = (t + 1) % 2
                ln_tile(x_in[(t + 1) * 128:(t + 2) * 128, :], 128, xbuf[bj],
                        b_xbuf[bj], xnbf[bj], b_xnbf[bj], ch_x[bj])
            bld.dma(ch_cc, xnb_scr[t * 128:(t + 1) * 128, :], xnb[:, :],
                    [bxnb], [b_xnscr])

            def gmmA(e, t=t, xnb=xnb):
                last = None
                for ci in range(8):
                    last = e.matmul(out=GB_A[ci][:, 0:512],
                                    lhsT=xnb[:, ci * 128:(ci + 1) * 128],
                                    rhs=xnb[:, 0:512],
                                    start=(t == 0), stop=(t == NT - 1))
                return last
            bld.on("tensor", [bxnb], b_GBA, gmmA)

        for ci in range(8):
            bld.on("vector", [b_GBA[ci]], [b_Gsb],
                   lambda e, ci=ci: e.tensor_copy(out=Gsb[:, ci * C:ci * C + 512],
                                                  in_=GB_A[ci][:, 0:512]))

        GB_B = [pp[0], pp[1], pS1, pS2]
        b_GBB = [b_pp[0], b_pp[1], b_pS1, b_pS2]
        for t in range(NT):
            bi = t % 2
            xnb, bxnb = xnbf[bi], b_xnbf[bi]
            bld.dma(ch_x[bi], xnb[:, :], xnb_scr[t * 128:(t + 1) * 128, :],
                    [b_xnscr], [bxnb])

            def gmmB(e, t=t, xnb=xnb):
                for k, ci in enumerate(range(4, 8)):
                    e.matmul(out=GB_B[k][:, 0:512],
                             lhsT=xnb[:, ci * 128:(ci + 1) * 128],
                             rhs=xnb[:, 512:1024],
                             start=(t == 0), stop=(t == NT - 1))
                last = None
                for half in range(2):
                    base = 64 * half
                    last = e.matmul(out=prs1[base:base + 1, 0:512],
                                    lhsT=ones_bf[:, 0:1],
                                    rhs=xnb[:, half * 512:(half + 1) * 512],
                                    start=(t == 0), stop=(t == NT - 1),
                                    tile_position=(0, base))
                return last
            bld.on("tensor", [bxnb, b_ones_bf], b_GBB + [b_prs1], gmmB)

        for k, ci in enumerate(range(4, 8)):
            bld.on("vector", [b_GBB[k]], [b_Gsb],
                   lambda e, k=k, ci=ci: e.tensor_copy(
                       out=Gsb[:, ci * C + 512:ci * C + 1024], in_=GB_B[k][:, 0:512]))
        bld.on("vector", [b_prs1], [b_xnsr],
               lambda e: e.tensor_copy(out=xnsr[0:1, 0:512], in_=prs1[0:1, 0:512]))
        bld.on("vector", [b_prs1, b_xnsr], [b_xnsr],
               lambda e: e.tensor_copy(out=xnsr[64:65, 0:512], in_=prs1[64:65, 0:512]))
        bld.on("vector", [b_xnsr], [b_xnsrb],
               lambda e: e.tensor_copy(out=xnsrb[0:1, 0:512], in_=xnsr[0:1, 0:512]))
        bld.on("vector", [b_xnsr, b_xnsrb], [b_xnsrb],
               lambda e: e.tensor_copy(out=xnsrb[0:1, 512:1024], in_=xnsr[64:65, 0:512]))

        for rnd in range(2):
            def mirt(e, rnd=rnd):
                last = None
                for k in range(8):
                    idx = rnd * 8 + k
                    i, j = idx // 4, 4 + idx % 4
                    last = e.transpose(out=ptr[0:128, k * 128:(k + 1) * 128],
                                       in_=Gsb[:, j * C + i * 128:j * C + (i + 1) * 128],
                                       identity=ident_bf[:, :])
                return last
            bld.on("tensor", [b_Gsb, b_ident_bf], [b_ptr], mirt)

            def mirc(e, rnd=rnd):
                last = None
                for k in range(8):
                    idx = rnd * 8 + k
                    i, j = idx // 4, 4 + idx % 4
                    last = e.tensor_copy(
                        out=Gsb[:, i * C + j * 128:i * C + (j + 1) * 128],
                        in_=ptr[0:128, k * 128:(k + 1) * 128])
                return last
            bld.on("vector", [b_ptr, b_Gsb], [b_Gsb], mirc)

        def xcol(e):
            last = None
            for ci in range(8):
                last = e.transpose(out=ptr[0:128, 2 * ci:2 * ci + 1],
                                   in_=xnsrb[0:1, ci * 128:(ci + 1) * 128],
                                   identity=ident_bf[0:1, 0:1])
            return last
        bld.on("tensor", [b_xnsrb, b_ident_bf], [b_ptr], xcol)
        bld.on("vector", [b_ptr], [b_xnscol],
               lambda e: e.tensor_copy(
                   out=xnscol[:, 0:8].rearrange("p (c k) -> p c k", k=1),
                   in_=ptr[0:128, 0:16].rearrange("p (c k) -> p c k", k=2)[:, :, 0:1]))

        for nm_, sl_ in (("Wv1", 1), ("Wk1", 0), ("Wv2", 3), ("Wk2", 2)):
            load_wT(sl_, nm_)

        for slot, nm_ in ((4, "bk1"), (5, "bv1"), (6, "bk2"), (7, "bv2")):
            bld.dma(ch_misc, rows2[0:1, slot * C:(slot + 1) * C],
                    inp[nm_].unsqueeze(0), [], [b_rows2])
        bld.on("vector", [b_rows2], [b_rows2],
               lambda e: e.tensor_single_scalar(out=rows2[0:1, 8 * C:9 * C],
                                                in_=rows2[0:1, 5 * C:6 * C],
                                                scalar=float(TOK), op=ALU.mult))
        bld.on("vector", [b_rows2], [b_rows2],
               lambda e: e.tensor_single_scalar(out=rows2[0:1, 9 * C:10 * C],
                                                in_=rows2[0:1, 7 * C:8 * C],
                                                scalar=float(TOK), op=ALU.mult))

        bld.dma(ch_miscf, p0pool[0:1, 0:1024], inp["bv1"].unsqueeze(0), [b_p0], [b_p0])
        bld.dma(ch_miscf, p0pool[0:1, 1024:2048], inp["bv2"].unsqueeze(0), [b_p0], [b_p0])

        for slot, wsl_ in ((0, 0), (1, 1), (2, 2), (3, 3)):
            for half in range(2):
                def umm(e, slot=slot, wsl_=wsl_, half=half):
                    for c2 in range(8):
                        e.matmul(out=pp[0][0:1, 0:512],
                                 lhsT=xnscol[:, c2:c2 + 1],
                                 rhs=wslot[wsl_][:, c2 * C + half * 512:c2 * C + half * 512 + 512],
                                 start=(c2 == 0), stop=False)
                    return e.matmul(out=pp[0][0:1, 0:512],
                                    lhsT=ones_bf[0:1, 0:1],
                                    rhs=xnsrb[0:1, half * 512:half * 512 + 512],
                                    start=False, stop=True,
                                    tile_position=(0, 0))
                bld.on("tensor", [b_xnscol, b_wslot[wsl_], b_xnsrb, b_ones_bf],
                       [b_pp[0]], umm)
                bld.on("vector", [b_pp[0]], [b_rows2],
                       lambda e, slot=slot, half=half: e.tensor_copy(
                           out=rows2[0:1, slot * C + half * 512:slot * C + half * 512 + 512],
                           in_=pp[0][0:1, 0:512]))
                if slot in (1, 3):
                    dst0 = (0 if slot == 1 else 1024) + half * 512
                    bld.on("vector", [b_pp[0], b_rowacc], [b_rowacc],
                           lambda e, dst0=dst0: e.tensor_copy(
                               out=rowacc[0:1, dst0:dst0 + 512], in_=pp[0][0:1, 0:512]))

        bld.on("vector", [b_rowacc, b_p0], [b_rowacc],
               lambda e: e.scalar_tensor_tensor(out=rowacc[0:1, 0:1024],
                                                in0=p0pool[0:1, 0:1024], scalar=float(TOK),
                                                in1=rowacc[0:1, 0:1024],
                                                op0=ALU.mult, op1=ALU.add))
        bld.on("vector", [b_rowacc, b_p0], [b_rowacc],
               lambda e: e.scalar_tensor_tensor(out=rowacc[0:1, 1024:2048],
                                                in0=p0pool[0:1, 1024:2048], scalar=float(TOK),
                                                in1=rowacc[0:1, 1024:2048],
                                                op0=ALU.mult, op1=ALU.add))

        def build_S(wsl_v, wsl_k, uk_slot, uv_slot, bk_slot, bv_slot, bvs_slot,
                    SXsb, b_SX):
            rotT = [(pp[0], b_pp[0]), (pp[1], b_pp[1]), (ptrf, b_ptrf)]
            ri = 0
            for c1 in range(8):
                for half in range(2):
                    bank, bbank = rotT[ri % 3]
                    ri += 1

                    def tmm(e, c1=c1, half=half, bank=bank, wsl_v=wsl_v):
                        for c2 in range(8):
                            e.matmul(out=bank[:, 0:512],
                                     lhsT=Gsb[:, c2 * C + c1 * 128:c2 * C + (c1 + 1) * 128],
                                     rhs=wslot[wsl_v][:, c2 * C + half * 512:c2 * C + half * 512 + 512],
                                     start=(c2 == 0), stop=False)
                        last = None
                        for k in range(4):
                            ce = half * 4 + k
                            last = e.matmul(out=bank[:, k * 128:(k + 1) * 128],
                                            lhsT=Gsb[:, ce * C + c1 * 128:ce * C + (c1 + 1) * 128],
                                            rhs=ident_bf[:, 0:128],
                                            start=False, stop=(k == 3))
                        return last
                    bld.on("tensor", [b_Gsb, b_wslot[wsl_v], b_ident_bf], [bbank], tmm)
                    bld.on("vector", [bbank], [b_Tsb],
                           lambda e, c1=c1, half=half, bank=bank: e.tensor_copy(
                               out=Tsb[:, c1 * C + half * 512:c1 * C + half * 512 + 512],
                               in_=bank[:, 0:512]))

            def smm2(e, wsl_k=wsl_k, uk_slot=uk_slot, uv_slot=uv_slot,
                     bk_slot=bk_slot, bv_slot=bv_slot, bvs_slot=bvs_slot):
                last = None
                for h in range(16):
                    r, cblk = 64 * (h % 2), 64 * (h // 2)
                    out_ap = pS2[r:r + 64, cblk:cblk + 64]
                    for c in range(8):
                        e.matmul(out=out_ap,
                                 lhsT=wslot[wsl_k][:, c * C + h * 64:c * C + h * 64 + 64],
                                 rhs=Tsb[:, c * C + h * 64:c * C + h * 64 + 64],
                                 start=(c == 0), stop=False)
                    e.matmul(out=out_ap,
                             lhsT=ident_bf[r:r + 64, r:r + 64],
                             rhs=Tsb[r:r + 64, (h // 2) * C + h * 64:(h // 2) * C + h * 64 + 64],
                             start=False, stop=False, tile_position=(r, r))
                    e.matmul(out=out_ap,
                             lhsT=rows2[0:1, uk_slot * C + h * 64:uk_slot * C + h * 64 + 64],
                             rhs=rows2[0:1, bv_slot * C + h * 64:bv_slot * C + h * 64 + 64],
                             start=False, stop=False, tile_position=(0, r))
                    e.matmul(out=out_ap,
                             lhsT=rows2[0:1, bk_slot * C + h * 64:bk_slot * C + h * 64 + 64],
                             rhs=rows2[0:1, uv_slot * C + h * 64:uv_slot * C + h * 64 + 64],
                             start=False, stop=False, tile_position=(0, r))
                    last = e.matmul(out=out_ap,
                                    lhsT=rows2[0:1, bk_slot * C + h * 64:bk_slot * C + h * 64 + 64],
                                    rhs=rows2[0:1, bvs_slot * C + h * 64:bvs_slot * C + h * 64 + 64],
                                    start=False, stop=True, tile_position=(0, r))
                return last
            bld.on("tensor", [b_wslot[wsl_k], b_Tsb, b_ident_bf, b_rows2], [b_pS2], smm2)
            bld.on("vector", [b_pS2], [b_SX],
                   lambda e, SXsb=SXsb: e.tensor_copy(out=SXsb[:], in_=pS2[:, 0:512]))

        build_S(1, 0, 0, 1, 4, 5, 8, S1sb, b_S1sb)
        build_S(3, 2, 2, 3, 6, 7, 9, S2sb, b_S2sb)

        if stage == 1:
            r = _finish_stub(nc, bld, es, y_out, xbuf, b_xbuf, ch_st, ch_x, x_in)
            es_a.close(); es_l0.close()
            return r
        b_ccin = Buf("ccin"); b_ccout = Buf("ccout")
        bld.dma(ch_cc, cc_in[0:128, :], S1sb[:], [b_S1sb], [b_ccin])
        bld.dma(ch_cc, cc_in[128:256, :], S2sb[:], [b_S2sb], [b_ccin])
        bld.dma(ch_cc, cc_in[256:257, :], rowacc[0:1, 0:512], [b_rowacc], [b_ccin])
        bld.dma(ch_cc, cc_in[257:258, :], rowacc[0:1, 512:1024], [b_rowacc], [b_ccin])
        bld.dma(ch_cc, cc_in[258:259, :], rowacc[0:1, 1024:1536], [b_rowacc], [b_ccin])
        bld.dma(ch_cc, cc_in[259:260, :], rowacc[0:1, 1536:2048], [b_rowacc], [b_ccin])

        rg = [[0, 1], [2, 3], [4, 5], [6, 7]]
        stream_g = bld.streams["gpsimd"]
        bld._emit_waits(stream_g, bld._dep_toks([b_ccin], [b_ccout]))

        def cc_item(eng, stream_g=stream_g):
            eng.collective_compute("AllReduce", ALU.add, replica_groups=rg,
                                   ins=[cc_in_t.ap().opt()], outs=[cc_out_t.ap().opt()]
                                   ).then_inc(stream_g.sem, 1)
        stream_g.items.append(("f", cc_item))
        stream_g.count += 1
        bld._track(stream_g, stream_g.count, [b_ccin], [b_ccout])

        S1f, b_S1f = S1sb, b_S1sb
        S2f, b_S2f = S2sb, b_S2sb
        bld.dma(ch_cc, S1f[:], cc_out[0:128, :], [b_ccout], [b_S1f])
        bld.dma(ch_cc, S2f[:], cc_out[128:256, :], [b_ccout], [b_S2f])
        bld.dma(ch_cc, vsum[0:1, 0:512], cc_out[256:257, :], [b_ccout], [b_vsum])
        bld.dma(ch_cc, vsum[0:1, 512:1024], cc_out[257:258, :], [b_ccout], [b_vsum])
        bld.dma(ch_cc, vsum[0:1, 1024:1536], cc_out[258:259, :], [b_ccout], [b_vsum])
        bld.dma(ch_cc, vsum[0:1, 1536:2048], cc_out[259:260, :], [b_ccout], [b_vsum])

        for i in range(1, 5):
            bld.dma(ch_miscf, gsb[0:1, (i - 1) * 16:i * 16],
                    inp[f"g{i}"].unsqueeze(0), [], [b_gsb])
        bld.dma(ch_miscf, gsb[64:65, 16:32], inp["g2"].unsqueeze(0), [], [b_gsb])

        def gpat_src(gi, r):
            return gsb[0:1, (gi - 1) * 16:gi * 16].rearrange(
                "p (c x) -> p c x", x=2)[:, :, r:r + 1].broadcast_to([1, 8, 64])

        def vrow_src(vbase, r):
            return vsum[0:1, vbase:vbase + 1024].rearrange(
                "p (c e) -> p c e", e=64)[:, r:16:2, :]

        def sbf_seg(base, r):
            return srow_bf[0:1, base + r * 512:base + (r + 1) * 512].rearrange(
                "p (c e) -> p c e", e=64)

        def build_vg(e, vbase, gi, dstbase):
            last = None
            for r in range(2):
                last = e.scalar_tensor_tensor(
                    out=sbf_seg(dstbase, r), in0=vrow_src(vbase, r),
                    scalar=1.0 / (NSEQ * NSEQ), in1=gpat_src(gi, r),
                    op0=ALU.mult, op1=ALU.mult)
            return last
        bld.on("vector", [b_vsum, b_gsb], [b_srow_bf], lambda e: build_vg(e, 0, 3, 0))
        bld.on("vector", [b_vsum, b_gsb], [b_srow_bf], lambda e: build_vg(e, 1024, 1, 1024))
        bld.on("vector", [b_gsb], [b_srow_bf],
               lambda e: [e.tensor_single_scalar(out=sbf_seg(2048, r), in_=gpat_src(4, r),
                                                 scalar=1.0 / (NSEQ * NSEQ), op=ALU.mult)
                          for r in range(2)][-1])
        def gpat_src64(gi, r):
            return gsb[64:65, (gi - 1) * 16:gi * 16].rearrange(
                "p (c x) -> p c x", x=2)[:, :, r:r + 1].broadcast_to([1, 8, 64])
        def g2build(e):
            e.tensor_single_scalar(
                out=p0pool[0:1, 0:512].rearrange("p (c e) -> p c e", e=64),
                in_=gpat_src(2, 0), scalar=1.0 / (NSEQ * NSEQ), op=ALU.mult)
            return e.tensor_single_scalar(
                out=p0pool[64:65, 0:512].rearrange("p (c e) -> p c e", e=64),
                in_=gpat_src64(2, 1), scalar=1.0 / (NSEQ * NSEQ), op=ALU.mult)
        bld.on("vector", [b_gsb], [b_p0], g2build)

        def bc(e, dstP, colbase):
            last = None
            for r in range(2):
                last = e.matmul(out=dstP[64 * r:64 * r + 64, 0:512],
                                lhsT=ones_bf[0:1, 0:64],
                                rhs=srow_bf[0:1, colbase + r * 512:colbase + (r + 1) * 512],
                                start=True, stop=True, tile_position=(0, 64 * r))
            return last
        bld.on("tensor", [b_srow_bf, b_ones_bf], [b_pS1], lambda e: bc(e, pS1, 0))
        bld.on("tensor", [b_srow_bf, b_ones_bf], [b_pS2], lambda e: bc(e, pS2, 1024))
        bld.on("vector", [b_S2f, b_pS1], [b_Mtmp],
               lambda e: e.tensor_tensor(out=Mtmp[:], in0=S2f[:], in1=pS1[:, 0:512], op=ALU.mult))
        bld.on("vector", [b_S1f, b_pS2], [b_sqscr],
               lambda e: e.tensor_tensor(out=sqscr[:, 0:512], in0=S1f[:],
                                         in1=pS2[:, 0:512], op=ALU.mult))
        bld.on("vector", [b_Mtmp, b_sqscr], [b_Mtmp],
               lambda e: e.tensor_tensor(out=Mtmp[:], in0=Mtmp[:],
                                         in1=sqscr[:, 0:512], op=ALU.add))
        bld.on("vector", [b_S1f, b_S2f], [b_sqscr],
               lambda e: e.tensor_tensor(out=sqscr[:, 0:512], in0=S1f[:], in1=S2f[:],
                                         op=ALU.mult))
        def y2mm(e):
            e.matmul(out=prs1[0:1, 0:512], lhsT=ones_f[0:64, 0:1],
                     rhs=sqscr[0:64, 0:512], start=True, stop=True)
            return e.matmul(out=prs2[64:65, 0:512], lhsT=ones_f[64:128, 0:1],
                            rhs=sqscr[64:128, 0:512], start=True, stop=True,
                            tile_position=(64, 64))
        bld.on("tensor", [b_sqscr, b_ones_f], [b_prs1, b_prs2], y2mm)
        def y2scale(e):
            e.tensor_tensor(out=y2bf[0:1, 0:512], in0=prs1[0:1, 0:512],
                            in1=p0pool[0:1, 0:512], op=ALU.mult)
            return e.tensor_tensor(out=y2bf[64:65, 0:512], in0=prs2[64:65, 0:512],
                                   in1=p0pool[64:65, 0:512], op=ALU.mult)
        bld.on("vector", [b_prs1, b_prs2, b_p0], [b_y2bf], y2scale)
        bld.on("tensor", [b_srow_bf, b_ones_bf], [b_pS1], lambda e: bc(e, pS1, 2048))
        bld.on("vector", [b_sqscr, b_pS1], [b_sqscr],
               lambda e: e.tensor_tensor(out=sqscr[:, 0:512], in0=sqscr[:, 0:512],
                                         in1=pS1[:, 0:512], op=ALU.mult))
        bld.on("vector", [b_Mtmp, b_sqscr], [b_Mtmp],
               lambda e: e.tensor_tensor(out=Mtmp[:], in0=Mtmp[:],
                                         in1=sqscr[:, 0:512], op=ALU.add))
        bld.on("vector", [b_Mtmp], [b_Mbf],
               lambda e: e.tensor_copy(out=Mbf[:], in_=Mtmp[:]))

        load_wT(0, "Wq")
        load_wT(1, "Wo0")
        load_wT(2, "Wf")

        bld.dma(ch_miscf, frow[0:1, :], inp["bq"].unsqueeze(0), [], [b_frow])
        bld.dma(ch_miscf, frow[32:33, :], inp["ln_g"].unsqueeze(0), [], [b_frow])
        bld.dma(ch_miscf, frow[64:65, :], inp["ln_b"].unsqueeze(0), [], [b_frow])

        def col_tr(e, p, dstcol):
            last = None
            for ci in range(8):
                last = e.transpose(out=ptrf[0:128, dstcol + ci:dstcol + ci + 1],
                                   in_=frow[p:p + 1, ci * 128:(ci + 1) * 128],
                                   identity=ident_f[p:p + 1, p:p + 1])
            return last
        for (p_, dc) in ((0, 0), (32, 8), (64, 16)):
            bld.on("tensor", [b_frow, b_ident_f], [b_ptrf],
                   lambda e, p_=p_, dc=dc: col_tr(e, p_, dc))
            bld.on("vector", [b_ptrf], [b_bcol],
                   lambda e, dc=dc: e.tensor_copy(out=bcol[:, dc:dc + 8],
                                                  in_=ptrf[:, dc:dc + 8]))
        bld.dma(ch_miscf, frow[0:1, :], inp["bo0"].unsqueeze(0), [b_bcol], [b_frow])

        bld.on("vector", [b_bcol], [b_lnbbf],
               lambda e: e.tensor_copy(out=lnbbf[:], in_=bcol[:, 16:24]))

        def bo2mm(e):
            last = None
            for half, bank in ((0, prs1), (1, prs2)):
                for ci in range(8):
                    last = e.matmul(out=bank[0:1, 0:512],
                                    lhsT=lnbbf[:, ci:ci + 1],
                                    rhs=wslot[1][:, ci * C + half * 512:ci * C + half * 512 + 512],
                                    start=(ci == 0), stop=(ci == 7))
            return last
        bld.on("tensor", [b_lnbbf, b_wslot[1]], [b_prs1, b_prs2], bo2mm)

        bld.dma(ch_miscf, p0pool[0:1, 1024:2048], inp["bf"].unsqueeze(0), [b_p0], [b_p0])

        def bias2a(e):
            e.tensor_tensor(out=p0pool[0:1, 0:512], in0=prs1[0:1, 0:512],
                            in1=frow[0:1, 0:512], op=ALU.add)
            return e.tensor_tensor(out=p0pool[0:1, 512:1024], in0=prs2[0:1, 0:512],
                                   in1=frow[0:1, 512:1024], op=ALU.add)
        bld.on("vector", [b_prs1, b_prs2, b_frow, b_p0], [b_p0], bias2a)
        bld.on("vector", [b_p0], [b_brow],
               lambda e: e.tensor_tensor(out=brow_ap("p2bias"), in0=p0pool[0:1, 0:1024],
                                         in1=p0pool[0:1, 1024:2048], op=ALU.add))

        def wo2scale(e):
            last = None
            for ci in range(8):
                last = e.tensor_scalar(out=wslot[1][:, ci * C:(ci + 1) * C],
                                       in0=wslot[1][:, ci * C:(ci + 1) * C],
                                       scalar1=bcol[:, 8 + ci:9 + ci], scalar2=None,
                                       op0=ALU.mult)
            return last
        bld.on("vector", [b_wslot[1], b_bcol], [b_wslot[1]], wo2scale)

        dTt2 = sbT(es_a, "s_dTt2", (128, 1024), BF16); b_dTt2 = Buf("dTt2")
        for ci in range(8):
            bld.dma(ch_w[0], dTt2[:, :], dT_scr[ci * 128:(ci + 1) * 128, :],
                    [b_dTscr], [b_dTt2])
            bld.on("vector", [b_dTt2, b_wslot[2]], [b_wslot[2]],
                   lambda e, ci=ci: e.tensor_tensor(
                       out=wslot[2][:, ci * C:(ci + 1) * C],
                       in0=dTt2[:, :], in1=wslot[2][:, ci * C:(ci + 1) * C], op=ALU.add))

        if debug:
            b_dbg = Buf("dbg")
            bld.dma(ch_st[0], dbg["d_S1"], S1f[:], [b_S1f], [b_dbg])
            bld.dma(ch_st[0], dbg["d_M"], Mtmp[:], [b_Mtmp], [b_dbg])

        bld.barrier()
        es_a.close()
        if stage == 2:
            r = _finish_stub(nc, bld, es, y_out, xbuf, b_xbuf, ch_st, ch_x, x_in)
            es_l0.close()
            return r

        es_b = ExitStack()
        qfm = sbT(es_b, "s_qfm", (128, 8 * 128), BF16); b_qfm = Buf("qfm")
        ylnbf = sbT(es_b, "s_ylnbf", (128, C), BF16); b_ylnbf = Buf("ylnbf")
        ylnfm = sbT(es_b, "s_ylnfm", (128, 8 * 128), BF16); b_ylnfm = Buf("ylnfm")
        xpfm = sbT(es_b, "s_xpfm", (128, 8 * PFX), BF16); b_xpfm = Buf("xpfm")
        x1b = [sbT(es_b, f"s_x1b{i}", (128, C), F32) for i in range(2)]
        b_x1b = [Buf(f"x1b{i}") for i in range(2)]
        fm2 = [sbT(es_b, f"s_fm2_{i}", (128, C), BF16) for i in range(2)]
        b_fm2 = [Buf(f"fm2_{i}") for i in range(2)]

        def p2_fm(t, width, is_prefix):
            if is_prefix:
                return (b_xpfm, lambda ci, w: xpfm[:, ci * PFX:ci * PFX + w])
            return (b_fm2[t % 2],
                    lambda ci, w, t=t: fm2[t % 2][:, ci * 128:ci * 128 + w])

        def fmprep(t):
            bi = t % 2
            xnb, bxnb = xnbf[bi], b_xnbf[bi]
            bld.dma(ch_x[bi], xnb[:, :], xnb_scr[t * 128:(t + 1) * 128, :],
                    [b_xnscr], [bxnb])

            def fmt(e, xnb=xnb):
                last = None
                for ci in range(8):
                    last = e.transpose(out=ptr[0:128, ci * 128:(ci + 1) * 128],
                                       in_=xnb[:, ci * 128:(ci + 1) * 128],
                                       identity=ident_bf[:, :])
                return last
            bld.on("tensor", [bxnb, b_ident_bf], [b_ptr], fmt)
            bld.on("vector", [b_ptr], [b_fm2[t % 2]],
                   lambda e, t=t: e.tensor_copy(out=fm2[t % 2][:, :],
                                                in_=ptr[0:128, 0:1024]))

        def p2_q(t, width, is_prefix, cos=range(8)):
            b_fm, fm_src = p2_fm(t, width, is_prefix)
            rotb = [(pp[0], b_pp[0]), (pp[1], b_pp[1]), (ptrf, b_ptrf)]
            for co in cos:
                pqb, bpqb = rotb[co % 3]

                def qmm(e, co=co, w=width, fm_src=fm_src, pqb=pqb):
                    last = None
                    for ci in range(8):
                        last = e.matmul(out=pqb[:, 0:w],
                                        lhsT=wslot[0][:, ci * C + co * 128:ci * C + (co + 1) * 128],
                                        rhs=fm_src(ci, w),
                                        start=(ci == 0), stop=(ci == 7))
                    return last
                bld.on("tensor", [b_fm, b_wslot[0]], [bpqb], qmm)

                def qev(e, co=co, w=width, pqb=pqb, fm_src=fm_src):
                    return e.scalar_tensor_tensor(
                        out=qfm[:, co * 128:co * 128 + w],
                        in0=pqb[:, 0:w], scalar=bcol[:, co:co + 1],
                        in1=fm_src(co, w), op0=ALU.add, op1=ALU.add)
                bld.on("vector", [bpqb, b_bcol, b_fm], [b_qfm], qev)

        def p2_restA(t, width, is_prefix):
            bi = t % 2
            xb, bxb = xbuf[bi], b_xbuf[bi]
            b_fm, fm_src = p2_fm(t, width, is_prefix)
            if not is_prefix:
                bld.dma(ch_x[bi], xb[:, :], x_in[t * 128:(t + 1) * 128, :], [], [bxb])

            def ymm(e, w=width):
                last = None
                for h in range(16):
                    r, cblk = h % 2, h // 2
                    bank = (pS1, pS2)[r]
                    coff = cblk * 64
                    e.matmul(out=bank[0:w, coff:coff + 64],
                             lhsT=qfm[64 * r:64 * r + 64, cblk * 128:cblk * 128 + w],
                             rhs=Mbf[64 * r:64 * r + 64, cblk * 64:cblk * 64 + 64],
                             start=True, stop=False)
                    last = e.matmul(out=bank[0:w, coff:coff + 64],
                                    lhsT=ones_bf[64 * r:64 * r + 1, 0:w],
                                    rhs=y2bf[64 * r:64 * r + 1, cblk * 64:cblk * 64 + 64],
                                    start=False, stop=True)
                return last
            bld.on("tensor", [b_qfm, b_Mbf, b_y2bf, b_ones_bf], [b_pS1, b_pS2], ymm)

        def p2_restLN(t, width, is_prefix):
            bi = t % 2

            b_rdA, b_rdB = Buf("rdA"), Buf("rdB")
            bld.on("scalar", [b_pS1], [b_sqscr, b_lnA, b_rdA],
                   lambda e, w=width: e.activation(
                       out=sqscr[0:w, 0:512], in_=pS1[0:w, 0:512], func=AF.Square,
                       accum_out=lnsq[0:w, 1:2]))
            bld.on("vector", [b_pS2], [b_lnst, b_rdB],
                   lambda e, w=width: e.reduce_sum(
                       out=lnst[0:w, 11:12], in_=pS2[0:w, 0:512], axis=AX))
            bld.on("scalar", [b_pS2, b_rdB], [b_sqscr, b_lnA],
                   lambda e, w=width: e.activation(
                       out=sqscr[0:w, 512:1024], in_=pS2[0:w, 0:512], func=AF.Square,
                       accum_out=lnsq[0:w, 2:3]))
            bld.on("vector", [b_pS1, b_rdA], [b_lnst],
                   lambda e, w=width: e.reduce_sum(
                       out=lnst[0:w, 10:11], in_=pS1[0:w, 0:512], axis=AX))

            bld.on("vector", [b_lnst], [b_lnst],
                   lambda e, w=width: e.tensor_tensor(
                       out=lnst[0:w, 10:11], in0=lnst[0:w, 10:11],
                       in1=lnst[0:w, 11:12], op=ALU.add))
            bld.on("vector", [b_lnst], [b_lnst],
                   lambda e, w=width: e.tensor_single_scalar(
                       out=lnst[0:w, 12:13], in_=lnst[0:w, 10:11],
                       scalar=1.0 / C, op=ALU.mult))
            bld.on("vector", [b_lnst], [b_lnst],
                   lambda e, w=width: e.tensor_scalar(
                       out=lnst[0:w, 13:14], in0=lnst[0:w, 10:11],
                       scalar1=lnst[0:w, 10:11], scalar2=1.0 / (C * C),
                       op0=ALU.mult, op1=ALU.mult))
            bld.on("vector", [b_lnA], [b_lnA],
                   lambda e, w=width: e.tensor_tensor(
                       out=lnsq[0:w, 1:2], in0=lnsq[0:w, 1:2],
                       in1=lnsq[0:w, 2:3], op=ALU.add))
            bld.on("vector", [b_lnst, b_lnA], [b_lnst],
                   lambda e, w=width: e.scalar_tensor_tensor(
                       out=lnst[0:w, 14:15], in0=lnsq[0:w, 1:2],
                       scalar=1.0 / C, in1=lnst[0:w, 13:14],
                       op0=ALU.mult, op1=ALU.subtract))

            bld.on("scalar", [b_lnst], [b_lnst],
                   lambda e, w=width: e.activation(out=lnst[0:w, 15:16],
                                                   in_=lnst[0:w, 14:15], func=AF.Ln,
                                                   bias=kvec[0:w, 0:1]))
            bld.on("scalar", [b_lnst], [b_lnst],
                   lambda e, w=width: e.activation(out=lnst[0:w, 14:15],
                                                   in_=lnst[0:w, 15:16],
                                                   func=AF.Exp, scale=-0.5))

            def yapply(e, w=width):
                dst0 = ylnbf[0:w, :].rearrange("p (h x) -> p h x", x=64)[:, 0:16:2, :]
                dst1 = ylnbf[0:w, :].rearrange("p (h x) -> p h x", x=64)[:, 1:16:2, :]
                e.tensor_scalar(out=dst0, in0=pS1[0:w, 0:512],
                                scalar1=lnst[0:w, 12:13], scalar2=lnst[0:w, 14:15],
                                op0=ALU.subtract, op1=ALU.mult)
                return e.tensor_scalar(out=dst1, in0=pS2[0:w, 0:512],
                                       scalar1=lnst[0:w, 12:13], scalar2=lnst[0:w, 14:15],
                                       op0=ALU.subtract, op1=ALU.mult)
            bld.on("vector", [b_pS1, b_pS2, b_lnst], [b_ylnbf], yapply)

        def p2_restB(t, width, is_prefix):
            bi = t % 2
            xb, bxb = xbuf[bi], b_xbuf[bi]
            b_fm, fm_src = p2_fm(t, width, is_prefix)
            fmdst2 = ylnfm[:, :].rearrange("p (ci t) -> p ci t", ci=8)[:, :, 0:width]
            transpose_tile_to_fm(ylnbf, b_ylnbf, fmdst2, b_ylnfm, width)

            for half, pob, bpob in ((0, prs1, b_prs1), (1, prs2, b_prs2)):
                def offmm(e, half=half, w=width, fm_src=fm_src, pob=pob):
                    for ci in range(8):
                        e.matmul(out=pob[0:w, 0:512],
                                 lhsT=ylnfm[:, ci * 128:ci * 128 + w],
                                 rhs=wslot[1][:, ci * C + half * 512:ci * C + half * 512 + 512],
                                 start=(ci == 0), stop=False)
                    for ci in range(8):
                        e.matmul(out=pob[0:w, 0:512],
                                 lhsT=fm_src(ci, w),
                                 rhs=wslot[2][:, ci * C + half * 512:ci * C + half * 512 + 512],
                                 start=False, stop=False)
                    bp = BRP["p2bias"][0]
                    return e.matmul(out=pob[0:w, 0:512],
                                    lhsT=ones_bf[bp:bp + 1, 0:w],
                                    rhs=brow_ap("p2bias", half * 512, 512),
                                    start=False, stop=True)
                bld.on("tensor", [b_ylnfm, b_fm, b_wslot[1], b_wslot[2], b_ones_bf, b_brow],
                       [bpob], offmm)

            if is_prefix:
                def x1ev(e, w=width, xb=xb):
                    e.tensor_tensor(out=x1p[0:w, 0:512], in0=prs1[0:w, 0:512],
                                    in1=xb[0:w, 0:512], op=ALU.add)
                    return e.tensor_tensor(out=x1p[0:w, 512:1024], in0=prs2[0:w, 0:512],
                                           in1=xb[0:w, 512:1024], op=ALU.add)
                bld.on("vector", [b_prs1, b_prs2, bxb], [b_x1p], x1ev)
            else:
                xo, bxo = x1b[bi], b_x1b[bi]

                def x1ev(e, w=width, xo=xo, xb=xb):
                    e.tensor_tensor(out=xo[0:w, 0:512], in0=prs1[0:w, 0:512],
                                    in1=xb[0:w, 0:512], op=ALU.add)
                    return e.tensor_tensor(out=xo[0:w, 512:1024], in0=prs2[0:w, 0:512],
                                           in1=xb[0:w, 512:1024], op=ALU.add)
                bld.on("vector", [b_prs1, b_prs2, bxb], [bxo], x1ev)
                bld.dma(ch_st[bi], x1_scr[t * 128:t * 128 + width, :], xo[0:width, :],
                        [bxo], [Buf(f"x1scr_t{t}")])

        xnbp, bxnbp = xnbf[0], b_xnbf[0]
        ln_tile(xp_in[0:PFX, :], PFX, xbuf[0], b_xbuf[0], xnbp, bxnbp, ch_x[0])
        fmdstp2 = xpfm[:, :].rearrange("p (ci t) -> p ci t", ci=8)[:, :, 0:PFX]
        transpose_tile_to_fm(xnbp, bxnbp, fmdstp2, b_xpfm, PFX)
        p2_q(0, PFX, True)
        p2_restA(0, PFX, True)
        p2_restLN(0, PFX, True)
        p2_restB(0, PFX, True)
        fmprep(0)
        p2_q(0, 128, False)
        for t in range(NT):
            if t + 1 < NT:
                fmprep(t + 1)
            p2_restA(t, 128, False)
            if t + 1 < NT:
                p2_q(t + 1, 128, False, cos=range(0, 3))
                p2_restLN(t, 128, False)
                p2_q(t + 1, 128, False, cos=range(3, 8))
            else:
                p2_restLN(t, 128, False)
            p2_restB(t, 128, False)

        bld.barrier()
        es_b.close()
        es_l0.close()

        if debug:
            b_dbg2 = Buf("dbg2")
            xtmp, bxt = xbuf[0], b_xbuf[0]
            for t in range(NT):
                bld.dma(ch_x[0], xtmp[:, :], x1_scr[t * 128:(t + 1) * 128, :], [bxt], [bxt])
                bld.dma(ch_st[0], dbg["d_x1"][t * 128:(t + 1) * 128, :], xtmp[:, :],
                        [bxt], [b_dbg2])

        if stage == 3:
            return _finish_stub(nc, bld, es, y_out, xbuf, b_xbuf, ch_st, ch_x, x_in,
                                src_scr=x1_scr)
        es_c = ExitStack()
        xn1g = sbT(es_c, "s_xn1g", (128, 8 * GL), BF16); b_xn1g = Buf("xn1g")
        qg = sbT(es_c, "s_qg", (128, 8 * GL), BF16); b_qg = Buf("qg")
        kg_ = sbT(es_c, "s_kg", (128, 8 * GL), BF16); b_kg = Buf("kg")
        v65 = sbT(es_c, "s_v65", (128, (GL // 128) * 16 * 65), BF16); b_v65 = Buf("v65")
        attg = sbT(es_c, "s_attg", (128, 8 * GL), BF16); b_attg = Buf("attg")
        accg = sbT(es_c, "s_accg", (128, (GL // 128) * C), F32); b_accg = Buf("accg")
        berow = sbT(es_c, "s_berow", (128, 3 * C), BF16); b_berow = Buf("berow")
        xp1fm = sbT(es_c, "s_xp1fm", (128, 8 * PFX), BF16); b_xp1fm = Buf("xp1fm")
        kgfm = sbT(es_c, "s_kgfm", (128, 8 * PFX), BF16); b_kgfm = Buf("kgfm")
        vg65 = sbT(es_c, "s_vg65", (64, 16 * 65), BF16); b_vg65 = Buf("vg65")
        pTl = sbT(es_c, "s_pTl", (128, 8 * 128), BF16); b_pTl = Buf("pTl")
        pTg = sbT(es_c, "s_pTg", (64, 8 * 128), BF16); b_pTg = Buf("pTg")
        zrec = sbT(es_c, "s_zrec", (128, 32), F32); b_zrec = Buf("zrec")
        atfm = sbT(es_c, "s_atfm", (128, 8 * 128), BF16); b_atfm = Buf("atfm")
        gmax = sbT(es_c, "s_gmax", (128, 16), F32); b_gmax = Buf("gmax")
        gm8 = sbT(es_c, "s_gm8", (128, 24), F32); b_gm8 = Buf("gm8")
        gidx = sbT(es_c, "s_gidx", (128, 24), U32); b_gidx = Buf("gidx")
        maskp = sbT(es_c, "s_maskp", (128, (GL // 128) * 8), F32); b_maskp = Buf("maskp")
        expv = sbT(es_c, "s_expv", (128, C), BF16); b_expv = Buf("expv")
        tmpe = sbT(es_c, "s_tmpe", (128, C), F32); b_tmpe = Buf("tmpe")
        tmpe2 = sbT(es_c, "s_tmpe2", (128, C), F32); b_tmpe2 = Buf("tmpe2")
        x2b = [sbT(es_c, f"s_x2b{i}", (128, C), F32) for i in range(2)]
        xbufH = sbT(es_c, "s_xbufH", (128, C), F32); b_xbufH = Buf("xbufH")
        xnbfH = sbT(es_c, "s_xnbfH", (128, C), BF16); b_xnbfH = Buf("xnbfH")
        b_x2b = [Buf(f"x2b{i}") for i in range(2)]

        BE = {}
        _beslots = [(p, c) for c in range(3) for p in (0, 32, 64)]
        for e_ in range(8):
            BE[e_] = _beslots[e_]
            p, cslot = BE[e_]
            bld.dma(ch_misc, berow[p:p + 1, cslot * C:(cslot + 1) * C],
                    inp["be"][e_:e_ + 1, :], [], [b_berow])

        def be_ap(e_, lo, ln):
            p, cslot = BE[e_]
            return berow[p:p + 1, cslot * C + lo:cslot * C + lo + ln]

        load_wT(0, "W1q")
        load_wT(1, "W1k")
        load_wT(2, "W1v")

        xnb, bxnb = xnbf[1], b_xnbf[1]
        bld.on("scalar", [b_x1p], [b_sqscr, b_lnst],
               lambda e: e.activation(out=sqscr[0:PFX, :], in_=x1p[0:PFX, :],
                                      func=AF.Square, accum_out=lnst[0:PFX, 1:2]))
        bld.on("vector", [b_x1p], [b_lnst],
               lambda e: e.reduce_sum(out=lnst[0:PFX, 0:1], in_=x1p[0:PFX, :], axis=AX))
        bld.on("vector", [b_lnst], [b_lnst],
               lambda e: e.tensor_single_scalar(out=lnst[0:PFX, 2:3], in_=lnst[0:PFX, 0:1],
                                                scalar=1.0 / C, op=ALU.mult))
        bld.on("vector", [b_lnst], [b_lnst],
               lambda e: e.tensor_tensor(out=lnst[0:PFX, 3:4], in0=lnst[0:PFX, 2:3],
                                         in1=lnst[0:PFX, 2:3], op=ALU.mult))
        bld.on("vector", [b_lnst], [b_lnst],
               lambda e: e.scalar_tensor_tensor(out=lnst[0:PFX, 4:5], in0=lnst[0:PFX, 1:2],
                                                scalar=1.0 / C, in1=lnst[0:PFX, 3:4],
                                                op0=ALU.mult, op1=ALU.subtract))
        bld.on("scalar", [b_lnst], [b_lnst],
               lambda e: e.activation(out=lnst[0:PFX, 5:6], in_=lnst[0:PFX, 4:5],
                                      func=AF.Ln, bias=kvec[0:PFX, 0:1]))
        bld.on("scalar", [b_lnst], [b_lnst],
               lambda e: e.activation(out=lnst[0:PFX, 6:7], in_=lnst[0:PFX, 5:6],
                                      func=AF.Exp, scale=-0.5))
        bld.on("vector", [b_x1p, b_lnst], [bxnb],
               lambda e: e.tensor_scalar(out=xnb[0:PFX, :], in0=x1p[0:PFX, :],
                                         scalar1=lnst[0:PFX, 2:3], scalar2=lnst[0:PFX, 6:7],
                                         op0=ALU.subtract, op1=ALU.mult))
        fmdstp = xp1fm[:, :].rearrange("p (ci t) -> p ci t", ci=8)[:, :, 0:PFX]
        transpose_tile_to_fm(xnb, bxnb, fmdstp, b_xp1fm, PFX)
        ln_tile(x1_scr[0:128, :], 128, xbufH, b_xbufH, xnbfH, b_xnbfH, ch_x[0])

        for co in range(8):
            pb, bpb = pp[co % 2], b_pp[co % 2]

            def kgmm(e, co=co, pb=pb):
                last = None
                for ci in range(8):
                    last = e.matmul(out=pb[:, 0:PFX],
                                    lhsT=wslot[1][:, ci * C + co * 128:ci * C + (co + 1) * 128],
                                    rhs=xp1fm[:, ci * PFX:(ci + 1) * PFX],
                                    start=(ci == 0), stop=(ci == 7))
                return last
            bld.on("tensor", [b_xp1fm, b_wslot[1]], [bpb], kgmm)
            bld.on("vector", [bpb], [b_kgfm],
                   lambda e, co=co, pb=pb: e.tensor_copy(
                       out=kgfm[:, co * PFX:(co + 1) * PFX], in_=pb[:, 0:PFX]))

        for half, pob, bpob in ((0, prs1, b_prs1), (1, prs2, b_prs2)):
            def vgmm(e, half=half, pob=pob):
                last = None
                for ci in range(8):
                    last = e.matmul(out=pob[0:PFX, 0:512],
                                    lhsT=xp1fm[:, ci * PFX:(ci + 1) * PFX],
                                    rhs=wslot[2][:, ci * C + half * 512:ci * C + half * 512 + 512],
                                    start=(ci == 0), stop=(ci == 7))
                return last
            bld.on("tensor", [b_xp1fm, b_wslot[2]], [bpob], vgmm)
        bld.on("vector", [b_ident_f], [b_vg65],
               lambda e: e.memset(vg65[0:64, :], 0.0))
        bld.on("vector", [b_vg65], [b_vg65],
               lambda e: e.memset(
                   vg65[0:64, :].rearrange("p (h x) -> p h x", x=65)[:, :, 64:65], 1.0))
        bld.on("vector", [b_prs1, b_vg65], [b_vg65],
               lambda e: e.tensor_copy(
                   out=vg65[0:64, :].rearrange("p (h x) -> p h x", x=65)[:, 0:8, 0:64],
                   in_=prs1[0:PFX, 0:512].rearrange("p (h d) -> p h d", d=64)))
        bld.on("vector", [b_prs2, b_vg65], [b_vg65],
               lambda e: e.tensor_copy(
                   out=vg65[0:64, :].rearrange("p (h x) -> p h x", x=65)[:, 8:16, 0:64],
                   in_=prs2[0:PFX, 0:512].rearrange("p (h d) -> p h d", d=64)))

        NSUB = GL // 128

        for g in range(NG):
            t0 = g * NSUB
            for st_ in range(NSUB):
                t = t0 + st_
                bi = t % 2
                if st_ == 0:
                    xnb2, bxnb2 = xnbfH, b_xnbfH
                else:
                    xnb2, bxnb2 = xnbf[bi], b_xnbf[bi]
                if st_ + 1 < NSUB:
                    bj = (t + 1) % 2
                    ln_tile(x1_scr[(t + 1) * 128:(t + 2) * 128, :], 128, xbuf[bj],
                            b_xbuf[bj], xnbf[bj], b_xnbf[bj], ch_x[bj])
                fmdst = xn1g[:, :].rearrange("p (ci t) -> p ci t", ci=8)[
                    :, :, st_ * 128:(st_ + 1) * 128]
                transpose_tile_to_fm(xnb2, bxnb2, fmdst, b_xn1g)

            rotb = [(pp[0], b_pp[0]), (pp[1], b_pp[1]), (ptrf, b_ptrf)]
            rot_i = 0
            for wslotid, dstt, b_dst in ((0, qg, b_qg), (1, kg_, b_kg)):
                for co in range(8):
                    pb, bpb = rotb[rot_i % 3]
                    rot_i += 1

                    def pmm(e, co=co, pb=pb, wslotid=wslotid):
                        last = None
                        for ci in range(8):
                            last = e.matmul(
                                out=pb[:, 0:GL],
                                lhsT=wslot[wslotid][:, ci * C + co * 128:ci * C + (co + 1) * 128],
                                rhs=xn1g[:, ci * GL:(ci + 1) * GL],
                                start=(ci == 0), stop=(ci == 7))
                        return last
                    bld.on("tensor", [b_xn1g, b_wslot[wslotid]], [bpb], pmm)
                    bld.on("vector", [bpb], [b_dst],
                           lambda e, co=co, pb=pb, dstt=dstt: e.tensor_copy(
                               out=dstt[:, co * GL:(co + 1) * GL], in_=pb[:, 0:GL]))

            if g == 0:
                bld.on("vector", [], [b_v65], lambda e: e.memset(v65[:, :], 0.0))
                bld.on("vector", [b_v65], [b_v65],
                       lambda e: e.memset(
                           v65[:, :].rearrange("p (s h x) -> p s h x", s=NSUB, x=65)[:, :, :, 64:65], 1.0))
            for st_ in range(NSUB):
                for half, pob, bpob in ((0, prs1, b_prs1), (1, prs2, b_prs2)):
                    def vmm(e, st_=st_, half=half, pob=pob):
                        last = None
                        for ci in range(8):
                            last = e.matmul(
                                out=pob[0:128, 0:512],
                                lhsT=xn1g[:, ci * GL + st_ * 128:ci * GL + (st_ + 1) * 128],
                                rhs=wslot[2][:, ci * C + half * 512:ci * C + half * 512 + 512],
                                start=(ci == 0), stop=(ci == 7))
                        return last
                    bld.on("tensor", [b_xn1g, b_wslot[2]], [bpob], vmm)
                    bld.on("vector", [bpob], [b_v65],
                           lambda e, st_=st_, half=half, pob=pob: e.tensor_copy(
                               out=v65[:, st_ * 16 * 65:(st_ + 1) * 16 * 65].rearrange(
                                   "p (h x) -> p h x", x=65)[:, half * 8:(half + 1) * 8, 0:64],
                               in_=pob[0:128, 0:512].rearrange("p (h d) -> p h d", d=64)))

            for st_ in (range(NSUB) if l1sub >= 2 else []):
                vloc = v65[:, st_ * 16 * 65:(st_ + 1) * 16 * 65]
                for hg in range(2):
                    def stmm(e, st_=st_, hg=hg):
                        last = None
                        for hh in range(8):
                            h = hg * 8 + hh
                            r, cblk = h % 2, h // 2
                            bank = pp[hh % 2]
                            coff = (hh // 2) * 128
                            last = e.matmul(
                                out=bank[:, coff:coff + 128],
                                lhsT=kg_[64 * r:64 * r + 64,
                                         cblk * GL + st_ * 128:cblk * GL + (st_ + 1) * 128],
                                rhs=qg[64 * r:64 * r + 64,
                                       cblk * GL + st_ * 128:cblk * GL + (st_ + 1) * 128],
                                start=True, stop=True)
                        return last
                    bld.on("tensor", [b_qg, b_kg], [b_pp[0], b_pp[1]], stmm)
                    if attnparts < 1:
                        continue

                    def sgmm(e, st_=st_, hg=hg):
                        last = None
                        for hh in range(8):
                            h = hg * 8 + hh
                            r, cblk = h % 2, h // 2
                            bank = (pS1, pS2)[hh % 2]
                            coff = (hh // 2) * 128
                            last = e.matmul(
                                out=bank[0:64, coff:coff + 128],
                                lhsT=kgfm[64 * r:64 * r + 64, cblk * PFX:(cblk + 1) * PFX],
                                rhs=qg[64 * r:64 * r + 64,
                                       cblk * GL + st_ * 128:cblk * GL + (st_ + 1) * 128],
                                start=True, stop=True)
                        return last
                    bld.on("tensor", [b_qg, b_kgfm], [b_pS1, b_pS2], sgmm)
                    if attnparts < 2:
                        continue

                    def expl(e, hg=hg):
                        e.activation(out=pTl[:, 0:512], in_=pp[0][:, 0:512], func=AF.Exp,
                                     scale=0.125, bias=kvec[:, 1:2])
                        return e.activation(out=pTl[:, 512:1024], in_=pp[1][:, 0:512],
                                            func=AF.Exp, scale=0.125, bias=kvec[:, 1:2])
                    bld.on("scalar", [b_pp[0], b_pp[1]], [b_pTl], expl)

                    def expg(e, hg=hg):
                        e.activation(out=pTg[0:64, 0:512], in_=pS1[0:64, 0:512], func=AF.Exp,
                                     scale=0.125, bias=kvec[0:64, 1:2])
                        return e.activation(out=pTg[0:64, 512:1024], in_=pS2[0:64, 0:512],
                                            func=AF.Exp, scale=0.125, bias=kvec[0:64, 1:2])
                    bld.on("scalar", [b_pS1, b_pS2], [b_pTg], expg)
                    if attnparts < 3:
                        continue

                    def avmm(e, st_=st_, hg=hg, vloc=vloc):
                        last = None
                        for hh in range(8):
                            h = hg * 8 + hh
                            bank = (prs1, prs2)[hh // 4]
                            coff = (hh % 4) * 65
                            pcol = (hh % 2) * 512 + (hh // 2) * 128
                            e.matmul(out=bank[:, coff:coff + 65],
                                     lhsT=pTl[:, pcol:pcol + 128],
                                     rhs=vloc[:, h * 65:(h + 1) * 65],
                                     start=True, stop=False)
                            last = e.matmul(out=bank[:, coff:coff + 65],
                                            lhsT=pTg[0:64, pcol:pcol + 128],
                                            rhs=vg65[0:64, h * 65:(h + 1) * 65],
                                            start=False, stop=True)
                        return last
                    bld.on("tensor", [b_pTl, b_pTg, b_v65, b_vg65],
                           [b_prs1, b_prs2], avmm)
                    if attnparts < 4:
                        continue

                    def normZ(e, st_=st_, hg=hg):
                        last = None
                        for bk, bank in ((0, prs1), (1, prs2)):
                            last = e.reciprocal(
                                out=zrec[:, hg * 8 + bk * 4:hg * 8 + bk * 4 + 4],
                                in_=bank[:, 0:260].rearrange("p (h x) -> p h x", x=65)[:, :, 64:65].rearrange("p h x -> p (h x)"))
                        return last
                    bld.on("vector", [b_prs1, b_prs2], [b_zrec], normZ)
                    def normM(e, st_=st_, hg=hg):
                        last = None
                        for bk, bank in ((0, prs1), (1, prs2)):
                            last = e.tensor_tensor(
                                out=attg[:, st_ * C + hg * 512 + bk * 256:
                                         st_ * C + hg * 512 + (bk + 1) * 256].rearrange(
                                             "p (h d) -> p h d", d=64),
                                in0=bank[:, 0:260].rearrange("p (h x) -> p h x", x=65)[:, :, 0:64],
                                in1=zrec[:, hg * 8 + bk * 4:hg * 8 + bk * 4 + 4].unsqueeze(2).broadcast_to(
                                    [128, 4, 64]),
                                op=ALU.mult)
                        return last
                    bld.on("vector", [b_prs1, b_prs2, b_zrec], [b_attg], normM)

            if l1sub >= 5:
                load_wT(1, "W1o", chi=g)
            if l1sub >= 3:
                load_wN(3, "Wg", chi=g)
            for st_ in (range(NSUB) if l1sub >= 3 else []):
                gbanks = (pp[0], pp[1]) if st_ % 2 == 0 else (pS1, pS2)
                gbufs = (b_pp[0], b_pp[1]) if st_ % 2 == 0 else (b_pS1, b_pS2)
                def gmm(e, st_=st_, gbanks=gbanks):
                    last = None
                    for half, bank in ((0, gbanks[0]), (1, gbanks[1])):
                        for ci in range(8):
                            e.matmul(out=bank[:, 0:512],
                                     lhsT=xn1g[:, ci * GL + st_ * 128:ci * GL + (st_ + 1) * 128],
                                     rhs=wslot[3][:, ci * C + half * 512:ci * C + half * 512 + 512],
                                     start=(ci == 0), stop=False)
                        bp = BRP["bg"][0]
                        last = e.matmul(out=bank[:, 0:512],
                                        lhsT=ones_bf[bp:bp + 1, 0:128],
                                        rhs=brow_ap("bg", half * 512, 512),
                                        start=False, stop=True)
                    return last
                bld.on("tensor", [b_xn1g, b_wslot[3], b_ones_bf, b_brow],
                       [gbufs[0], gbufs[1]], gmm)

                def gstats1(e, st_=st_, gbanks=gbanks):
                    e.reduce_max(out=gmax[:, 0:1], in_=gbanks[0][:, 0:512], axis=AX)
                    return e.reduce_max(out=gmax[:, 1:2], in_=gbanks[1][:, 0:512], axis=AX)
                bld.on("vector", [gbufs[0], gbufs[1]], [b_gmax], gstats1)
                bld.on("vector", [b_gmax], [b_gmax],
                       lambda e: e.tensor_tensor(out=gmax[:, 2:3], in0=gmax[:, 0:1],
                                                 in1=gmax[:, 1:2], op=ALU.max))
                bld.on("vector", [b_gmax], [b_gmax],
                       lambda e: e.tensor_single_scalar(out=gmax[:, 3:4], in_=gmax[:, 2:3],
                                                        scalar=-1.0, op=ALU.mult))

                def gexp(e, st_=st_, gbanks=gbanks):
                    e.activation(out=expv[:, 0:512], in_=gbanks[0][:, 0:512], func=AF.Exp,
                                 bias=gmax[:, 3:4], accum_out=gmax[:, 4:5])
                    return e.activation(out=expv[:, 512:1024], in_=gbanks[1][:, 0:512],
                                        func=AF.Exp, bias=gmax[:, 3:4], accum_out=gmax[:, 5:6])
                bld.on("scalar", [gbufs[0], gbufs[1], b_gmax], [b_expv, b_gmax], gexp)

                bld.on("vector", [b_gmax], [b_gmax],
                       lambda e: e.tensor_tensor(out=gmax[:, 6:7], in0=gmax[:, 4:5],
                                                 in1=gmax[:, 5:6], op=ALU.add))
                bld.on("vector", [b_gmax], [b_gmax],
                       lambda e: e.reciprocal(out=gmax[:, 7:8], in_=gmax[:, 6:7]))

                def gidxB2(e, st_=st_, gbanks=gbanks):
                    e.max_index(out=gidx[:, 0:8], in_max=gm8[:, 8:16],
                                in_values=gbanks[0][:, 0:512])
                    return e.max_index(out=gidx[:, 8:16], in_max=gm8[:, 16:24],
                                       in_values=gbanks[1][:, 0:512])
                def gidxA2(e, st_=st_):
                    e.tensor_copy(out=gm8[:, 8:16], in_=gmax[:, 0:1].broadcast_to([128, 8]))
                    return e.tensor_copy(out=gm8[:, 16:24],
                                         in_=gmax[:, 1:2].broadcast_to([128, 8]))
                bld.on("vector", [b_gmax], [b_gm8], gidxA2)
                bld.on("vector", [gbufs[0], gbufs[1], b_gm8], [b_gidx], gidxB2)
                def gidxC(e, st_=st_):
                    e.tensor_copy(out=gmax[:, 8:9], in_=gidx[:, 0:1])
                    return e.tensor_copy(out=gmax[:, 9:10], in_=gidx[:, 8:9])
                bld.on("vector", [b_gidx], [b_gmax], gidxC)
                bld.on("vector", [b_gmax], [b_gmax],
                       lambda e: e.tensor_tensor(out=gmax[:, 12:13], in0=gmax[:, 1:2],
                                                 in1=gmax[:, 0:1], op=ALU.is_gt))
                bld.on("vector", [b_gmax], [b_gmax],
                       lambda e: e.tensor_tensor(out=gmax[:, 13:14], in0=gmax[:, 9:10],
                                                 in1=gmax[:, 8:9], op=ALU.subtract))
                bld.on("vector", [b_gmax], [b_gmax],
                       lambda e: e.scalar_tensor_tensor(out=gmax[:, 10:11],
                                                        in0=gmax[:, 12:13], scalar=1.0,
                                                        in1=gmax[:, 13:14],
                                                        op0=ALU.mult, op1=ALU.mult))
                bld.on("vector", [b_gmax], [b_gmax],
                       lambda e: e.tensor_tensor(out=gmax[:, 10:11], in0=gmax[:, 10:11],
                                                 in1=gmax[:, 8:9], op=ALU.add))
                bld.on("vector", [b_gmax], [b_gidx],
                       lambda e: e.tensor_copy(out=gidx[:, 16:17], in_=gmax[:, 10:11]))
                bld.on("vector", [b_gidx], [b_gidx],
                       lambda e: e.tensor_single_scalar(out=gidx[:, 17:18], in_=gidx[:, 16:17],
                                                        scalar=7, op=ALU.bitwise_and))
                bld.on("vector", [b_gidx], [b_gmax],
                       lambda e: e.tensor_copy(out=gmax[:, 11:12], in_=gidx[:, 17:18]))

                bld.on("vector", [b_gmax, b_iota8], [b_maskp],
                       lambda e, st_=st_: e.tensor_tensor(
                           out=maskp[:, st_ * 8:(st_ + 1) * 8],
                           in0=gmax[:, 11:12].broadcast_to([128, 8]),
                           in1=iota8[:, 0:8], op=ALU.is_equal))
                bld.on("vector", [b_gmax, b_maskp], [b_maskp],
                       lambda e, st_=st_: e.tensor_scalar(
                           out=maskp[:, st_ * 8:(st_ + 1) * 8],
                           in0=maskp[:, st_ * 8:(st_ + 1) * 8],
                           scalar1=gmax[:, 7:8], scalar2=None, op0=ALU.mult))

            if g + 1 < NG:
                tn = (g + 1) * NSUB
                ln_tile(x1_scr[tn * 128:(tn + 1) * 128, :], 128, xbufH, b_xbufH,
                        xnbfH, b_xnbfH, ch_x[0])
            for e_ in (range(8) if l1sub >= 4 else []):
                wsl = 3 if e_ % 2 == 0 else 0
                load_wN(wsl, f"We{e_}", chi=e_)
                for st_ in range(NSUB):
                    t = t0 + st_
                    bi = st_ % 2
                    ebanks = (pp[0], pp[1]) if st_ % 2 == 0 else (pS1, pS2)
                    ebufs = (b_pp[0], b_pp[1]) if st_ % 2 == 0 else (b_pS1, b_pS2)
                    etmp, betmp = (tmpe, b_tmpe) if st_ % 2 == 0 else (tmpe2, b_tmpe2)
                    def hmm(e, st_=st_, wsl=wsl, e_=e_, ebanks=ebanks):
                        last = None
                        for half, bank in ((0, ebanks[0]), (1, ebanks[1])):
                            for ci in range(8):
                                e.matmul(out=bank[:, 0:512],
                                         lhsT=xn1g[:, ci * GL + st_ * 128:ci * GL + (st_ + 1) * 128],
                                         rhs=wslot[wsl][:, ci * C + half * 512:ci * C + half * 512 + 512],
                                         start=(ci == 0), stop=False)
                            bp = BE[e_][0]
                            last = e.matmul(out=bank[:, 0:512],
                                            lhsT=ones_bf[bp:bp + 1, 0:128],
                                            rhs=be_ap(e_, half * 512, 512),
                                            start=False, stop=True)
                        return last
                    bld.on("tensor", [b_xn1g, b_wslot[wsl], b_ones_bf, b_berow],
                           [ebufs[0], ebufs[1]], hmm)

                    def sel(e, st_=st_, e_=e_, ebanks=ebanks, etmp=etmp):
                        mp = maskp[:, st_ * 8 + e_:st_ * 8 + e_ + 1].broadcast_to([128, 512])
                        e.scalar_tensor_tensor(out=etmp[:, 0:512], in0=ebanks[0][:, 0:512],
                                               scalar=0.0, in1=mp,
                                               op0=ALU.max, op1=ALU.mult)
                        return e.scalar_tensor_tensor(out=etmp[:, 512:1024], in0=ebanks[1][:, 0:512],
                                                      scalar=0.0, in1=mp,
                                                      op0=ALU.max, op1=ALU.mult)
                    bld.on("vector", [ebufs[0], ebufs[1], b_maskp], [betmp], sel)

                    if e_ == 0:
                        xb, bxb = xbuf[bi], b_xbuf[bi]
                        bld.dma(ch_x[bi], xb[:, :], x1_scr[t * 128:(t + 1) * 128, :], [], [bxb])
                        bld.on("gpsimd", [betmp, bxb], [b_accg],
                               lambda e, st_=st_, xb=xb, etmp=etmp: e.tensor_tensor(
                                   out=accg[:, st_ * C:(st_ + 1) * C], in0=etmp[:, :],
                                   in1=xb[:, :], op=ALU.add))
                    else:
                        bld.on("gpsimd", [betmp, b_accg], [b_accg],
                               lambda e, st_=st_, etmp=etmp: e.tensor_tensor(
                                   out=accg[:, st_ * C:(st_ + 1) * C], in0=etmp[:, :],
                                   in1=accg[:, st_ * C:(st_ + 1) * C], op=ALU.add))

            for st_ in (range(NSUB) if l1sub >= 5 else []):
                t = t0 + st_
                bi = st_ % 2
                att_t = attg[:, st_ * C:(st_ + 1) * C]

                def atmm(e, att_t=att_t):
                    last = None
                    for ci in range(8):
                        last = e.transpose(out=ptr[0:128, ci * 128:(ci + 1) * 128],
                                           in_=att_t[:, ci * 128:(ci + 1) * 128],
                                           identity=ident_bf[:, :])
                    return last
                bld.on("tensor", [b_attg, b_ident_bf], [b_ptr], atmm)
                bld.on("vector", [b_ptr], [b_atfm],
                       lambda e: e.tensor_copy(out=atfm[:, :], in_=ptr[0:128, 0:1024]))

                for half, pob, bpob in ((0, prs1, b_prs1), (1, prs2, b_prs2)):
                    def wo1mm(e, half=half, pob=pob):
                        last = None
                        for ci in range(8):
                            last = e.matmul(
                                out=pob[0:128, 0:512],
                                lhsT=atfm[:, ci * 128:(ci + 1) * 128],
                                rhs=wslot[1][:, ci * C + half * 512:ci * C + half * 512 + 512],
                                start=(ci == 0), stop=(ci == 7))
                        return last
                    bld.on("tensor", [b_atfm, b_wslot[1]], [bpob], wo1mm)

                xo, bxo = x2b[bi], b_x2b[bi]

                def x2ev(e, st_=st_, xo=xo):
                    e.tensor_tensor(out=xo[:, 0:512], in0=prs1[:, 0:512],
                                    in1=accg[:, st_ * C:st_ * C + 512], op=ALU.add)
                    return e.tensor_tensor(out=xo[:, 512:1024], in0=prs2[:, 0:512],
                                           in1=accg[:, st_ * C + 512:(st_ + 1) * C], op=ALU.add)
                bld.on("vector", [b_prs1, b_prs2, b_accg], [bxo], x2ev)
                bld.dma(ch_st[bi], y_out[t * 128:(t + 1) * 128, :], xo[:, :],
                        [bxo], [Buf(f"y_t{t}")])

            if g + 1 < NG:
                load_wT(0, "W1q", chi=g)
                load_wT(1, "W1k", chi=g + 1)

        if l1sub < 5:
            bld.barrier()
            r = _finish_stub(nc, bld, es, y_out, xbuf, b_xbuf, ch_st, ch_x, x_in,
                             src_scr=x1_scr)
            es_c.close()
            return r
        for chn in ch_st:
            bld.wait_chan("gpsimd", chn)

        bld.alloc_sems(es)
        block = es.enter_context(nc.Block())
        bld.emit(block)
        es_c.close()
    return nc, dbg



def _consts():
    import ml_dtypes
    return {
        "ident_bf": np.eye(128, dtype=ml_dtypes.bfloat16),
        "ident_f32": np.eye(128, dtype=np.float32),
        "ones_bf": np.ones((128, 128), dtype=ml_dtypes.bfloat16),
        "ones_f32": np.ones((128, 128), dtype=np.float32),
        "iota8": np.tile(np.arange(8, dtype=np.float32), (128, 1)),
        "kvec": np.tile(np.array([1e-5, -2.5, 0.0, 0.0], dtype=np.float32), (128, 1)),
    }


def make_in_maps(**inputs):
    x = np.ascontiguousarray(inputs["L0_attn_x"], dtype=np.float32)
    consts = _consts()
    common = {
        "Wq": inputs["L0_attn_Wq"], "bq": inputs["L0_attn_bq"],
        "Wk1": inputs["L0_attn_Wk1"], "bk1": inputs["L0_attn_bk1"],
        "Wk2": inputs["L0_attn_Wk2"], "bk2": inputs["L0_attn_bk2"],
        "Wv1": inputs["L0_attn_Wv1"], "bv1": inputs["L0_attn_bv1"],
        "Wv2": inputs["L0_attn_Wv2"], "bv2": inputs["L0_attn_bv2"],
        "g1": inputs["L0_attn_g1"], "g2": inputs["L0_attn_g2"],
        "g3": inputs["L0_attn_g3"], "g4": inputs["L0_attn_g4"],
        "ln_g": inputs["L0_attn_ln_g"], "ln_b": inputs["L0_attn_ln_b"],
        "Wo0": inputs["L0_attn_Wo"], "bo0": inputs["L0_attn_bo"],
        "Wf": inputs["L0_ffn_W"], "bf": inputs["L0_ffn_b"],
        "lora_down": inputs["L0_ffn_lora_down"], "lora_up": inputs["L0_ffn_lora_up"],
        "down_aux": inputs["L0_ffn_down_aux"], "up_aux": inputs["L0_ffn_up_aux"],
        "W1q": inputs["L1_attn_Wq"], "W1k": inputs["L1_attn_Wk"],
        "W1v": inputs["L1_attn_Wv"], "W1o": inputs["L1_attn_Wo"],
        "Wg": inputs["L1_ffn_Wg"], "bg": inputs["L1_ffn_bg"],
        "We": inputs["L1_ffn_We"], "be": inputs["L1_ffn_be"],
    }
    common = {k: np.ascontiguousarray(np.asarray(v), dtype=np.float32)
              for k, v in common.items()}
    in_maps = []
    for c in range(NCORE):
        b, half = c >> 1, c & 1
        m = dict(common)
        m.update(consts)
        m["x"] = np.ascontiguousarray(x[b, half * TOK:(half + 1) * TOK, :])
        m["xp"] = np.ascontiguousarray(x[b, 0:PFX, :])
        in_maps.append(m)
    return in_maps


_cache = {}

def kernel(**inputs):
    if "nc" not in _cache:
        _cache["nc"], _ = build_program(debug=DEBUG_OUTPUTS, stage=STAGE, l1sub=L1SUB, attnparts=ATTNPARTS)
    nc = _cache["nc"]
    in_maps = make_in_maps(**inputs)
    res = run_bass_kernel_spmd(nc, in_maps, core_ids=list(range(NCORE)))
    out = np.empty((B, T, C), dtype=np.float32)
    for c in range(NCORE):
        b, half = c >> 1, c & 1
        out[b, half * TOK:(half + 1) * TOK, :] = res.results[c]["y"]
    return out


# version 16 (final)
# speedup vs baseline: 1.1560x; 1.1560x over previous
import numpy as np
from contextlib import ExitStack

import concourse.bass as bass
import concourse.mybir as mybir
from concourse.bass_utils import run_bass_kernel_spmd

F32 = mybir.dt.float32
BF16 = mybir.dt.bfloat16
U32 = mybir.dt.uint32
AF = mybir.ActivationFunctionType
ALU = mybir.AluOpType
AX = mybir.AxisListType.X

B, T, C, H, D = 4, 8192, 1024, 16, 64
NCORE = 8
TOK = T * B // NCORE
NT = TOK // 128
PFX = 64
NSEQ = float(T)
LN_EPS = 1e-5
SOFT_SHIFT = 2.5
GL = 512
NG = TOK // GL

DEBUG_OUTPUTS = False
STAGE = 4
L1SUB = 5
ATTNPARTS = 4



class _Stream:
    def __init__(self, name):
        self.name = name
        self.sem = None
        self.count = 0
        self.waited = {}
        self.items = []


class Tok:
    __slots__ = ("stream", "value")
    def __init__(self, stream, value):
        self.stream = stream
        self.value = value


class Buf:
    __slots__ = ("name", "writer", "readers")
    def __init__(self, name=""):
        self.name = name
        self.writer = None
        self.readers = {}


class Chan:
    def __init__(self, name, stream):
        self.name = name
        self.stream = stream
        self.sem = None
        self.count = 0


_SEEN_INC = {}


class Bld:
    ENGINES = ("tensor", "vector", "scalar", "gpsimd", "sync")

    def __init__(self, nc):
        self.nc = nc
        self.streams = {e: _Stream(e) for e in self.ENGINES}
        self.chans = []

    def chan(self, name, stream_name):
        c = Chan(name, self.streams[stream_name])
        self.chans.append(c)
        return c

    def _dep_toks(self, reads, writes):
        toks = []
        for b in reads:
            if b.writer is not None:
                toks.append(b.writer)
        for b in writes:
            if b.writer is not None:
                toks.append(b.writer)
            for s, v in b.readers.items():
                toks.append(Tok(s, v))
        return toks

    def _emit_waits(self, stream, toks):
        need = {}
        for t in toks:
            src = t.stream
            if src is stream and stream.name == "tensor":
                continue
            if t.value > need.get(src, 0):
                need[src] = t.value
        for src, v in need.items():
            if v > stream.waited.get(src, 0):
                stream.waited[src] = v
                stream.items.append(("w", src, v))

    def _track(self, src, count, reads, writes):
        tok = Tok(src, count)
        for b in reads:
            if count > b.readers.get(src, 0):
                b.readers[src] = count
        for b in writes:
            b.writer = tok
            b.readers = {}
        return tok

    def on(self, stream_name, reads, writes, fn):
        stream = self.streams[stream_name]
        self._emit_waits(stream, self._dep_toks(reads, writes))

        import traceback
        tb = traceback.extract_stack()[-3:-1]
        cnt = stream.count + 1

        def item(eng, fn=fn, stream=stream, tb=tb, cnt=cnt):
            last = fn(eng)
            assert last is not None, "step fn must return its last instruction"
            key = id(last)
            if key in _SEEN_INC:
                raise RuntimeError(
                    f"step #{cnt} on {stream.name} returned an instruction already "
                    f"inc'd by {_SEEN_INC[key]}; this step from {tb}")
            _SEEN_INC[key] = (stream.name, cnt, str(tb), last)
            last.then_inc(stream.sem, 1)
        stream.items.append(("f", item))
        stream.count += 1
        return self._track(stream, stream.count, reads, writes)

    def dma(self, chan, out, in_, reads, writes, **kw):
        stream = chan.stream
        if chan.count > 0:
            self.wait_chan(stream.name, chan)
        self._emit_waits(stream, self._dep_toks(reads, writes))

        def item(eng, out=out, in_=in_, chan=chan, kw=kw):
            eng.dma_start(out=out, in_=in_, **kw).then_inc(chan.sem, 16)
        stream.items.append(("f", item))
        chan.count += 16
        return self._track(chan, chan.count, reads, writes)

    def wait_chan(self, stream_name, chan, value=None):
        stream = self.streams[stream_name]
        v = chan.count if value is None else value
        if v > stream.waited.get(chan, 0):
            stream.waited[chan] = v
            stream.items.append(("w", chan, v))

    def barrier(self, skip_streams=(), skip_chans=()):
        snap = [(s, s.count) for s in self.streams.values()
                if s.count > 0 and s.name not in skip_streams]
        snapc = [(c, c.count) for c in self.chans
                 if c.count > 0 and c not in skip_chans]
        for s in self.streams.values():
            for src, v in snap:
                if src is s:
                    continue
                if v > s.waited.get(src, 0):
                    s.waited[src] = v
                    s.items.append(("w", src, v))
            for ch, v in snapc:
                if v > s.waited.get(ch, 0):
                    s.waited[ch] = v
                    s.items.append(("w", ch, v))

    def alloc_sems(self, es):
        for s in self.streams.values():
            s.sem = es.enter_context(self.nc.semaphore(f"sem_{s.name}"))
        for c in self.chans:
            c.sem = es.enter_context(self.nc.semaphore(f"ch_{c.name}"))

    def emit(self, block):
        for name in self.ENGINES:
            s = self.streams[name]
            items = s.items

            def run(eng, items=items):
                for it in items:
                    if it[0] == "w":
                        eng.wait_ge(it[1].sem, it[2])
                    else:
                        it[1](eng)
            getattr(block, name)(run)



def _finish_stub(nc, bld, es, y_out, xbuf, b_xbuf, ch_st, ch_x, x_in, src_scr=None):
    for t in range(NT):
        bi = t % 2
        xb, bxb = xbuf[bi], b_xbuf[bi]
        src = (src_scr if src_scr is not None else x_in)[t * 128:(t + 1) * 128, :]
        bld.dma(ch_x[bi], xb[:, :], src, [], [bxb])
        bld.dma(ch_st[bi], y_out[t * 128:(t + 1) * 128, :], xb[:, :], [bxb],
                [Buf(f"ystub_{t}")])
    for chn in ch_st:
        bld.wait_chan("gpsimd", chn)
    bld.alloc_sems(es)
    block = es.enter_context(nc.Block())
    bld.emit(block)
    return nc, {}

def build_program(debug=False, stage=4, l1sub=5, attnparts=4):
    nc = bass.Bass("TRN2", target_bir_lowering=False)

    inp = {}
    def ein(name, shape, dtype=F32):
        inp[name] = nc.dram_tensor(name, list(shape), dtype, kind="ExternalInput").ap()
        return inp[name]

    x_in = ein("x", (TOK, C))
    xp_in = ein("xp", (PFX, C))
    for nm in ("q", "k1", "k2", "v1", "v2"):
        ein(f"W{nm}", (C, C)); ein(f"b{nm}", (C,))
    for i in range(1, 5):
        ein(f"g{i}", (H,))
    ein("ln_g", (C,)); ein("ln_b", (C,))
    ein("Wo0", (C, C)); ein("bo0", (C,))
    ein("Wf", (C, C)); ein("bf", (C,))
    ein("lora_down", (4, 100)); ein("lora_up", (50, 4))
    ein("down_aux", (100, C)); ein("up_aux", (C, 50))
    ein("W1q", (C, C)); ein("W1k", (C, C)); ein("W1v", (C, C)); ein("W1o", (C, C))
    ein("Wg", (C, C)); ein("bg", (C,))
    ein("We", (8, C, C)); ein("be", (8, C))
    ein("ident_bf", (128, 128), BF16)
    ein("ident_f32", (128, 128), F32)
    ein("ones_bf", (128, 128), BF16)
    ein("ones_f32", (128, 128), F32)
    ein("iota8", (128, 8), F32)
    ein("kvec", (128, 4), F32)

    y_out = nc.dram_tensor("y", [TOK, C], F32, kind="ExternalOutput").ap()
    dbg = {}
    if debug:
        def dout(name, shape, dtype=F32):
            dbg[name] = nc.dram_tensor(name, list(shape), dtype, kind="ExternalOutput").ap()
        dout("d_x1", (TOK, C))
        dout("d_S1", (128, 512))
        dout("d_M", (128, 512))

    wt_names = ["Wq", "Wk1", "Wk2", "Wv1", "Wv2", "Wo0", "Wf",
                "W1q", "W1k", "W1v", "W1o", "Wg"] + [f"We{e}" for e in range(8)]
    wtbf = {n: nc.dram_tensor(f"wtbf_{n}", [C, C], BF16, kind="Internal").ap()
            for n in wt_names}
    dT_scr = nc.dram_tensor("dT_scr", [C, C], BF16, kind="Internal").ap()
    wtbfT = {n: nc.dram_tensor(f"wtbfT_{n}", [128, 8 * C], BF16, kind="Internal").ap()
             for n in ("W1q", "W1k", "W1o")}
    x1_scr = nc.dram_tensor("x1_scr", [TOK, C], F32, kind="Internal").ap()
    xnb_scr = nc.dram_tensor("xnb_scr", [TOK, C], BF16, kind="Internal").ap()
    cc_in_t = nc.dram_tensor("cc_in", [260, 512], F32, kind="Internal")
    cc_out_t = nc.dram_tensor("cc_out", [260, 512], F32, kind="Internal")
    cc_in = cc_in_t.ap()
    cc_out = cc_out_t.ap()

    bld = Bld(nc)
    es = ExitStack()
    with es:
        ch_w = [bld.chan(f"w{i}", "sync") for i in range(2)]
        ch_wc = bld.chan("wc", "gpsimd")
        ch_x = [bld.chan(f"x{i}", "sync") for i in range(2)]
        ch_misc = bld.chan("misc", "gpsimd")
        ch_miscf = bld.chan("miscf", "sync")
        ch_st = [bld.chan(f"st{i}", "gpsimd") for i in range(2)]
        ch_cc = bld.chan("cc", "sync")

        def sbT(stack, name, shape, dt=F32):
            return stack.enter_context(nc.sbuf_tensor(name, list(shape), dt))

        pp = [es.enter_context(nc.psum_tensor(f"p_pp{i}", [128, 512], F32)) for i in range(2)]
        b_pp = [Buf(f"pp{i}") for i in range(2)]
        pS1 = es.enter_context(nc.psum_tensor("p_S1", [128, 512], F32)); b_pS1 = Buf("pS1")
        pS2 = es.enter_context(nc.psum_tensor("p_S2", [128, 512], F32)); b_pS2 = Buf("pS2")
        p8 = es.enter_context(nc.psum_tensor("p_p8", [128, 512], F32)); b_ptr = Buf("p8")
        ptr = p8[0:128, 0:512].bitcast(BF16)
        prs1 = es.enter_context(nc.psum_tensor("p_rs1", [128, 512], F32)); b_prs1 = Buf("prs1")
        prs2 = es.enter_context(nc.psum_tensor("p_rs2", [128, 512], F32)); b_prs2 = Buf("prs2")
        ptrf = es.enter_context(nc.psum_tensor("p_trf", [128, 512], F32)); b_ptrf = Buf("ptrf")

        ident_bf = sbT(es, "s_ident_bf", (128, 128), BF16); b_ident_bf = Buf("ident_bf")
        ident_f = sbT(es, "s_ident_f", (128, 128), F32); b_ident_f = Buf("ident_f")
        ones_bf = sbT(es, "s_ones_bf", (128, 128), BF16); b_ones_bf = Buf("ones_bf")
        ones_f = sbT(es, "s_ones_f", (128, 128), F32); b_ones_f = Buf("ones_f")
        iota8 = sbT(es, "s_iota8", (128, 8), F32); b_iota8 = Buf("iota8")
        kvec = sbT(es, "s_kvec", (128, 4), F32); b_kvec = Buf("kvec")

        NWS = 4
        wslot = [sbT(es, f"s_wslot{i}", (128, 8 * C), BF16) for i in range(NWS)]
        b_wslot = [Buf(f"wslot{i}") for i in range(NWS)]

        brow = sbT(es, "s_brow", (128, 4 * C), BF16); b_brow = Buf("brow")
        BRP = {}
        _brslots = [(p, c) for c in range(4) for p in (0, 32, 64)]

        def brow_ap(name, lo=0, ln=C):
            p, c = BRP[name]
            return brow[p:p + 1, c * C + lo:c * C + lo + ln]

        bcol = sbT(es, "s_bcol", (128, 32), F32); b_bcol = Buf("bcol")
        xbuf = [sbT(es, f"s_xbuf{i}", (128, C), F32) for i in range(2)]
        b_xbuf = [Buf(f"xbuf{i}") for i in range(2)]
        xnbf = [sbT(es, f"s_xnbf{i}", (128, C), BF16) for i in range(4)]
        b_xnbf = [Buf(f"xnbf{i}") for i in range(4)]
        lnst = sbT(es, "s_lnst", (128, 16), F32); b_lnst = Buf("lnst")
        lnsq = sbT(es, "s_lnsq", (128, 4), F32)
        sqscr = sbT(es, "s_sqscr", (128, C), F32); b_sqscr = Buf("sqscr")
        x1p = sbT(es, "s_x1p", (64, C), F32); b_x1p = Buf("x1p")

        bld.dma(ch_miscf, ident_f[:], inp["ident_f32"], [], [b_ident_f])
        bld.dma(ch_miscf, ones_f[:], inp["ones_f32"], [], [b_ones_f])
        bld.dma(ch_miscf, iota8[:], inp["iota8"], [], [b_iota8])
        bld.dma(ch_miscf, kvec[:], inp["kvec"], [], [b_kvec])
        bld.dma(ch_misc, ident_bf[:], inp["ident_bf"], [], [b_ident_bf])
        bld.dma(ch_misc, ones_bf[:], inp["ones_bf"], [], [b_ones_bf])

        def load_brow(name, src=None):
            BRP[name] = _brslots[len(BRP)]
            src = src if src is not None else inp[name].unsqueeze(0)
            p, c = BRP[name]
            bld.dma(ch_misc, brow[p:p + 1, c * C:(c + 1) * C], src, [], [b_brow])
        for nm in ("bq", "bk1", "bk2", "bv1", "bv2", "bg"):
            load_brow(nm)
        BRP["p2bias"] = _brslots[len(BRP)]
        assert BRP["p2bias"][0] == 0 or True
        if BRP["p2bias"][0] != 0:
            for s in _brslots[len(BRP):]:
                if s[0] == 0:
                    BRP["p2bias"] = s
                    break

        b_wtbf = {n: Buf(f"wtbf_{n}") for n in wt_names}
        wsrc = {f"We{e}": inp["We"][e] for e in range(8)}
        for n in ("Wv1", "Wk1", "Wv2", "Wk2"):
            bld.dma(ch_wc, wtbf[n], wsrc.get(n, inp.get(n)), [], [b_wtbf[n]])

        b_wtbfT = {k: Buf(f"wtbfT_{k}") for k in wtbfT}
        _wtbfT_saved = set()

        def load_wT(slotid, name, chi=None):
            ch = ch_w[(slotid if chi is None else chi) % 2]
            if name in wtbfT and name in _wtbfT_saved:
                bld.dma(ch, wslot[slotid][:, :], wtbfT[name],
                        [b_wtbfT[name]], [b_wslot[slotid]])
                return
            for ci in range(8):
                bld.dma(ch, wslot[slotid][:, ci * C:(ci + 1) * C],
                        wtbf[name][0:C, ci * 128:(ci + 1) * 128],
                        [b_wtbf[name]], [b_wslot[slotid]], transpose=True)
            if name in wtbfT:
                bld.dma(ch, wtbfT[name], wslot[slotid][:, :],
                        [b_wslot[slotid]], [b_wtbfT[name]])
                _wtbfT_saved.add(name)

        def load_wN(slotid, name, chi=None):
            ch = ch_w[(slotid if chi is None else chi) % 2]
            src = wtbf[name].rearrange("(r p) c -> p r c", p=128)
            bld.dma(ch, wslot[slotid][:, :].rearrange("p (r c) -> p r c", c=C), src,
                    [b_wtbf[name]], [b_wslot[slotid]])

        es_pro = ExitStack()
        ldsb = sbT(es_pro, "s_ldsb", (128, 128), F32); b_ldsb = Buf("ldsb")
        lusb = sbT(es_pro, "s_lusb", (128, 8), F32); b_lusb = Buf("lusb")
        luT_sb = sbT(es_pro, "s_luT", (128, 64), F32); b_luT = Buf("luT")
        ldT_sb = sbT(es_pro, "s_ldT", (128, 8), F32); b_ldT = Buf("ldT")
        dasb = sbT(es_pro, "s_dasb", (128, 1024), F32); b_dasb = Buf("dasb")
        uasb = sbT(es_pro, "s_uasb", (128, 400), F32); b_uasb = Buf("uasb")
        P1sb = sbT(es_pro, "s_P1sb", (128, 1024), F32); b_P1sb = Buf("P1sb")
        P2sb = sbT(es_pro, "s_P2sb", (128, 1024), F32); b_P2sb = Buf("P2sb")
        uaT = sbT(es_pro, "s_uaT", (128, 1024), F32); b_uaT = Buf("uaT")
        dTt = [sbT(es_pro, f"s_dTt{i}", (128, 1024), BF16) for i in range(2)]
        b_dTt = [Buf(f"dTt{i}") for i in range(2)]

        bld.dma(ch_miscf, ldsb[0:4, 0:100], inp["lora_down"], [], [b_ldsb])
        bld.dma(ch_miscf, lusb[0:50, 0:4], inp["lora_up"], [], [b_lusb])
        bld.dma(ch_miscf, dasb[0:100, :], inp["down_aux"], [], [b_dasb])
        bld.dma(ch_miscf, uasb[:, :].rearrange("p (r k) -> p r k", k=50),
                inp["up_aux"].rearrange("(r p) k -> p r k", p=128),
                [], [b_uasb])

        psm, b_psm = pp[0], b_pp[0]
        bld.on("tensor", [b_ldsb, b_ident_f], [b_psm],
               lambda e: e.transpose(out=psm[0:100, 0:4], in_=ldsb[0:4, 0:100],
                                     identity=ident_f[0:4, 0:4]))
        bld.on("vector", [b_psm], [b_ldT],
               lambda e: e.tensor_copy(out=ldT_sb[0:100, 0:4], in_=psm[0:100, 0:4]))
        for half in range(2):
            psmH, b_psmH = (pp[half], b_pp[half])
            bld.on("tensor", [b_ldT, b_dasb], [b_psmH],
                   lambda e, half=half, psmH=psmH: e.matmul(
                       out=psmH[0:4, 0:512], lhsT=ldT_sb[0:100, 0:4],
                       rhs=dasb[0:100, half * 512:(half + 1) * 512],
                       start=True, stop=True))
            bld.on("vector", [b_psmH], [b_P1sb],
                   lambda e, half=half, psmH=psmH: e.tensor_copy(
                       out=P1sb[0:4, half * 512:(half + 1) * 512], in_=psmH[0:4, 0:512]))
        bld.on("tensor", [b_lusb, b_ident_f], [b_psm],
               lambda e: e.transpose(out=psm[0:4, 0:50], in_=lusb[0:50, 0:4],
                                     identity=ident_f[0:50, 0:50]))
        bld.on("vector", [b_psm], [b_luT],
               lambda e: e.tensor_copy(out=luT_sb[0:4, 0:50], in_=psm[0:4, 0:50]))
        for half in range(2):
            psmH, b_psmH = (pp[half], b_pp[half])
            bld.on("tensor", [b_luT, b_P1sb], [b_psmH],
                   lambda e, half=half, psmH=psmH: e.matmul(
                       out=psmH[0:50, 0:512], lhsT=luT_sb[0:4, 0:50],
                       rhs=P1sb[0:4, half * 512:(half + 1) * 512],
                       start=True, stop=True))
            bld.on("vector", [b_psmH], [b_P2sb],
                   lambda e, half=half, psmH=psmH: e.tensor_copy(
                       out=P2sb[0:50, half * 512:(half + 1) * 512], in_=psmH[0:50, 0:512]))
        for r in range(8):
            psmH, b_psmH = (pp[r % 2], b_pp[r % 2])
            bld.on("tensor", [b_uasb, b_ident_f], [b_psmH],
                   lambda e, r=r, psmH=psmH: e.transpose(out=psmH[0:50, 0:128],
                                                         in_=uasb[:, r * 50:(r + 1) * 50],
                                                         identity=ident_f[:, :]))
            bld.on("vector", [b_psmH], [b_uaT],
                   lambda e, r=r, psmH=psmH: e.tensor_copy(out=uaT[0:50, r * 128:(r + 1) * 128],
                                                           in_=psmH[0:50, 0:128]))
        b_dTscr = Buf("dTscr")
        for ci in range(8):
            bsel = ci % 2
            psmX, b_psmX = pp[bsel], b_pp[bsel]
            for half in range(2):
                bld.on("tensor", [b_P2sb, b_uaT], [b_psmX],
                       lambda e, ci=ci, half=half, psmX=psmX: e.matmul(
                           out=psmX[0:128, 0:512],
                           lhsT=P2sb[0:50, ci * 128:(ci + 1) * 128],
                           rhs=uaT[0:50, half * 512:(half + 1) * 512],
                           start=True, stop=True))
                bld.on("vector", [b_psmX], [b_dTt[bsel]],
                       lambda e, half=half, psmX=psmX, bsel=bsel: e.tensor_copy(
                           out=dTt[bsel][:, half * 512:(half + 1) * 512],
                           in_=psmX[0:128, 0:512]))
            bld.dma(ch_cc, dT_scr[ci * 128:(ci + 1) * 128, :], dTt[bsel][:, :],
                    [b_dTt[bsel]], [b_dTscr])
        bld.barrier(skip_streams=("gpsimd",), skip_chans=(ch_wc, ch_misc))
        es_pro.close()
        if stage == 0:
            return _finish_stub(nc, bld, es, y_out, xbuf, b_xbuf, ch_st, ch_x, x_in)
        for n in ["Wq", "Wo0", "Wf", "W1q", "W1k", "W1v", "Wg", "W1o"] + \
                [f"We{e}" for e in range(8)]:
            bld.dma(ch_wc, wtbf[n], wsrc.get(n, inp.get(n)), [], [b_wtbf[n]])

        es_l0 = ExitStack()
        Mbf = sbT(es_l0, "s_Mbf", (128, 512), BF16); b_Mbf = Buf("Mbf")
        y2bf = sbT(es_l0, "s_y2bf", (128, 512), BF16); b_y2bf = Buf("y2bf")

        es_a = ExitStack()
        S1sb = sbT(es_a, "s_S1sb", (128, 512), F32); b_S1sb = Buf("S1sb")
        S2sb = sbT(es_a, "s_S2sb", (128, 512), F32); b_S2sb = Buf("S2sb")
        rowacc = sbT(es_a, "s_rowacc", (128, 2048), F32); b_rowacc = Buf("rowacc")
        Gsb = sbT(es_a, "s_Gsb", (128, 8 * C), BF16); b_Gsb = Buf("Gsb")
        Tsb = sbT(es_a, "s_Tsb", (128, 8 * C), BF16); b_Tsb = Buf("Tsb")
        rows2 = sbT(es_a, "s_rows2", (128, 10 * C), BF16); b_rows2 = Buf("rows2")
        xnsr = sbT(es_a, "s_xnsr", (128, 512), F32); b_xnsr = Buf("xnsr")
        xnsrb = sbT(es_a, "s_xnsrb", (1, 1024), BF16); b_xnsrb = Buf("xnsrb")
        xnscol = sbT(es_a, "s_xnscol", (128, 16), BF16); b_xnscol = Buf("xnscol")
        b_xnscr = Buf("xnscr")
        vsum = sbT(es_a, "s_vsum", (128, 2048), F32); b_vsum = Buf("vsum")
        p0pool = sbT(es_a, "s_p0pool", (128, 2048), F32); b_p0 = Buf("p0pool")
        srow_bf = sbT(es_a, "s_srow_bf", (128, 3072), BF16); b_srow_bf = Buf("srow_bf")
        gsb = sbT(es_a, "s_gsb", (128, 64), F32); b_gsb = Buf("gsb")
        frow = sbT(es_a, "s_frow", (128, 1024), F32); b_frow = Buf("frow")
        Mtmp = sbT(es_a, "s_Mtmp", (128, 512), F32); b_Mtmp = Buf("Mtmp")
        lnbbf = sbT(es_a, "s_lnbbf", (128, 8), BF16); b_lnbbf = Buf("lnbbf")

        b_lnA = Buf("lnA")

        def ln_tile(x_src, width, xb, b_xb, xnb, b_xnb, chan):
            bld.dma(chan, xb[0:width, :], x_src, [], [b_xb])
            bld.on("scalar", [b_xb], [b_sqscr, b_lnA],
                   lambda e, xb=xb, w=width: e.activation(
                       out=sqscr[0:w, :], in_=xb[0:w, :], func=AF.Square,
                       accum_out=lnsq[0:w, 0:1]))
            bld.on("vector", [b_xb], [b_lnst],
                   lambda e, xb=xb, w=width: e.reduce_sum(
                       out=lnst[0:w, 0:1], in_=xb[0:w, :], axis=AX))
            bld.on("vector", [b_lnst], [b_lnst],
                   lambda e, w=width: e.tensor_single_scalar(
                       out=lnst[0:w, 2:3], in_=lnst[0:w, 0:1], scalar=1.0 / C, op=ALU.mult))
            bld.on("vector", [b_lnst], [b_lnst],
                   lambda e, w=width: e.tensor_scalar(
                       out=lnst[0:w, 3:4], in0=lnst[0:w, 0:1],
                       scalar1=lnst[0:w, 0:1], scalar2=1.0 / (C * C),
                       op0=ALU.mult, op1=ALU.mult))
            bld.on("vector", [b_lnst, b_lnA], [b_lnst],
                   lambda e, w=width: e.scalar_tensor_tensor(
                       out=lnst[0:w, 4:5], in0=lnsq[0:w, 0:1], scalar=1.0 / C,
                       in1=lnst[0:w, 3:4], op0=ALU.mult, op1=ALU.subtract))
            bld.on("scalar", [b_lnst], [b_lnA],
                   lambda e, w=width: e.activation(
                       out=lnst[0:w, 5:6], in_=lnst[0:w, 4:5], func=AF.Ln, bias=kvec[0:w, 0:1]))
            bld.on("scalar", [b_lnA], [b_lnA],
                   lambda e, w=width: e.activation(
                       out=lnst[0:w, 6:7], in_=lnst[0:w, 5:6], func=AF.Exp, scale=-0.5))
            bld.on("vector", [b_xb, b_lnst, b_lnA], [b_xnb],
                   lambda e, xb=xb, xnb=xnb, w=width: e.tensor_scalar(
                       out=xnb[0:w, :], in0=xb[0:w, :],
                       scalar1=lnst[0:w, 2:3], scalar2=lnst[0:w, 6:7],
                       op0=ALU.subtract, op1=ALU.mult))

        def transpose_tile_to_fm(xnb, b_xnb, dst_ap, b_dst, width=128):
            def mmfn(e, xnb=xnb, w=width):
                last = None
                for ci in range(8):
                    last = e.transpose(out=ptr[0:128, ci * 128:ci * 128 + w],
                                       in_=xnb[0:w, ci * 128:(ci + 1) * 128],
                                       identity=ident_bf[0:w, 0:w])
                return last
            bld.on("tensor", [b_xnb, b_ident_bf], [b_ptr], mmfn)

            def cpfn(e, dst_ap=dst_ap, w=width):
                src = ptr[0:128, :].rearrange("p (ci t) -> p ci t", ci=8)[:, :, 0:w]
                return e.tensor_copy(out=dst_ap, in_=src)
            bld.on("vector", [b_ptr], [b_dst], cpfn)

        GB_A = [pp[0], pp[1], pS1, pS2, p8, prs1, prs2, ptrf]
        b_GBA = [b_pp[0], b_pp[1], b_pS1, b_pS2, b_ptr, b_prs1, b_prs2, b_ptrf]

        ln_tile(x_in[0:128, :], 128, xbuf[0], b_xbuf[0], xnbf[0], b_xnbf[0], ch_x[0])
        for t in range(NT):
            bi = t % 2
            xnb, bxnb = xnbf[bi], b_xnbf[bi]
            if t + 1 < NT:
                bj = (t + 1) % 2
                ln_tile(x_in[(t + 1) * 128:(t + 2) * 128, :], 128, xbuf[bj],
                        b_xbuf[bj], xnbf[bj], b_xnbf[bj], ch_x[bj])
            bld.dma(ch_cc, xnb_scr[t * 128:(t + 1) * 128, :], xnb[:, :],
                    [bxnb], [b_xnscr])

            def gmmA(e, t=t, xnb=xnb):
                last = None
                for ci in range(8):
                    last = e.matmul(out=GB_A[ci][:, 0:512],
                                    lhsT=xnb[:, ci * 128:(ci + 1) * 128],
                                    rhs=xnb[:, 0:512],
                                    start=(t == 0), stop=(t == NT - 1))
                return last
            bld.on("tensor", [bxnb], b_GBA, gmmA)

        for ci in range(8):
            bld.on("vector", [b_GBA[ci]], [b_Gsb],
                   lambda e, ci=ci: e.tensor_copy(out=Gsb[:, ci * C:ci * C + 512],
                                                  in_=GB_A[ci][:, 0:512]))

        GB_B = [pp[0], pp[1], pS1, pS2]
        b_GBB = [b_pp[0], b_pp[1], b_pS1, b_pS2]
        for t in range(NT):
            bi = t % 2
            xnb, bxnb = xnbf[bi], b_xnbf[bi]
            bld.dma(ch_x[bi], xnb[:, :], xnb_scr[t * 128:(t + 1) * 128, :],
                    [b_xnscr], [bxnb])

            def gmmB(e, t=t, xnb=xnb):
                for k, ci in enumerate(range(4, 8)):
                    e.matmul(out=GB_B[k][:, 0:512],
                             lhsT=xnb[:, ci * 128:(ci + 1) * 128],
                             rhs=xnb[:, 512:1024],
                             start=(t == 0), stop=(t == NT - 1))
                last = None
                for half in range(2):
                    base = 64 * half
                    last = e.matmul(out=prs1[base:base + 1, 0:512],
                                    lhsT=ones_bf[:, 0:1],
                                    rhs=xnb[:, half * 512:(half + 1) * 512],
                                    start=(t == 0), stop=(t == NT - 1),
                                    tile_position=(0, base))
                return last
            bld.on("tensor", [bxnb, b_ones_bf], b_GBB + [b_prs1], gmmB)

        for k, ci in enumerate(range(4, 8)):
            bld.on("vector", [b_GBB[k]], [b_Gsb],
                   lambda e, k=k, ci=ci: e.tensor_copy(
                       out=Gsb[:, ci * C + 512:ci * C + 1024], in_=GB_B[k][:, 0:512]))
        bld.on("vector", [b_prs1], [b_xnsr],
               lambda e: e.tensor_copy(out=xnsr[0:1, 0:512], in_=prs1[0:1, 0:512]))
        bld.on("vector", [b_prs1, b_xnsr], [b_xnsr],
               lambda e: e.tensor_copy(out=xnsr[64:65, 0:512], in_=prs1[64:65, 0:512]))
        bld.on("vector", [b_xnsr], [b_xnsrb],
               lambda e: e.tensor_copy(out=xnsrb[0:1, 0:512], in_=xnsr[0:1, 0:512]))
        bld.on("vector", [b_xnsr, b_xnsrb], [b_xnsrb],
               lambda e: e.tensor_copy(out=xnsrb[0:1, 512:1024], in_=xnsr[64:65, 0:512]))

        for rnd in range(2):
            def mirt(e, rnd=rnd):
                last = None
                for k in range(8):
                    idx = rnd * 8 + k
                    i, j = idx // 4, 4 + idx % 4
                    last = e.transpose(out=ptr[0:128, k * 128:(k + 1) * 128],
                                       in_=Gsb[:, j * C + i * 128:j * C + (i + 1) * 128],
                                       identity=ident_bf[:, :])
                return last
            bld.on("tensor", [b_Gsb, b_ident_bf], [b_ptr], mirt)

            def mirc(e, rnd=rnd):
                last = None
                for k in range(8):
                    idx = rnd * 8 + k
                    i, j = idx // 4, 4 + idx % 4
                    last = e.tensor_copy(
                        out=Gsb[:, i * C + j * 128:i * C + (j + 1) * 128],
                        in_=ptr[0:128, k * 128:(k + 1) * 128])
                return last
            bld.on("vector", [b_ptr, b_Gsb], [b_Gsb], mirc)

        def xcol(e):
            last = None
            for ci in range(8):
                last = e.transpose(out=ptr[0:128, 2 * ci:2 * ci + 1],
                                   in_=xnsrb[0:1, ci * 128:(ci + 1) * 128],
                                   identity=ident_bf[0:1, 0:1])
            return last
        bld.on("tensor", [b_xnsrb, b_ident_bf], [b_ptr], xcol)
        bld.on("vector", [b_ptr], [b_xnscol],
               lambda e: e.tensor_copy(
                   out=xnscol[:, 0:8].rearrange("p (c k) -> p c k", k=1),
                   in_=ptr[0:128, 0:16].rearrange("p (c k) -> p c k", k=2)[:, :, 0:1]))

        for nm_, sl_ in (("Wv1", 1), ("Wk1", 0), ("Wv2", 3), ("Wk2", 2)):
            load_wT(sl_, nm_)

        for slot, nm_ in ((4, "bk1"), (5, "bv1"), (6, "bk2"), (7, "bv2")):
            bld.dma(ch_misc, rows2[0:1, slot * C:(slot + 1) * C],
                    inp[nm_].unsqueeze(0), [], [b_rows2])
        bld.on("vector", [b_rows2], [b_rows2],
               lambda e: e.tensor_single_scalar(out=rows2[0:1, 8 * C:9 * C],
                                                in_=rows2[0:1, 5 * C:6 * C],
                                                scalar=float(TOK), op=ALU.mult))
        bld.on("vector", [b_rows2], [b_rows2],
               lambda e: e.tensor_single_scalar(out=rows2[0:1, 9 * C:10 * C],
                                                in_=rows2[0:1, 7 * C:8 * C],
                                                scalar=float(TOK), op=ALU.mult))

        bld.dma(ch_miscf, p0pool[0:1, 0:1024], inp["bv1"].unsqueeze(0), [b_p0], [b_p0])
        bld.dma(ch_miscf, p0pool[0:1, 1024:2048], inp["bv2"].unsqueeze(0), [b_p0], [b_p0])

        for slot, wsl_ in ((0, 0), (1, 1), (2, 2), (3, 3)):
            for half in range(2):
                def umm(e, slot=slot, wsl_=wsl_, half=half):
                    for c2 in range(8):
                        e.matmul(out=pp[0][0:1, 0:512],
                                 lhsT=xnscol[:, c2:c2 + 1],
                                 rhs=wslot[wsl_][:, c2 * C + half * 512:c2 * C + half * 512 + 512],
                                 start=(c2 == 0), stop=False)
                    return e.matmul(out=pp[0][0:1, 0:512],
                                    lhsT=ones_bf[0:1, 0:1],
                                    rhs=xnsrb[0:1, half * 512:half * 512 + 512],
                                    start=False, stop=True,
                                    tile_position=(0, 0))
                bld.on("tensor", [b_xnscol, b_wslot[wsl_], b_xnsrb, b_ones_bf],
                       [b_pp[0]], umm)
                bld.on("vector", [b_pp[0]], [b_rows2],
                       lambda e, slot=slot, half=half: e.tensor_copy(
                           out=rows2[0:1, slot * C + half * 512:slot * C + half * 512 + 512],
                           in_=pp[0][0:1, 0:512]))
                if slot in (1, 3):
                    dst0 = (0 if slot == 1 else 1024) + half * 512
                    bld.on("vector", [b_pp[0], b_rowacc], [b_rowacc],
                           lambda e, dst0=dst0: e.tensor_copy(
                               out=rowacc[0:1, dst0:dst0 + 512], in_=pp[0][0:1, 0:512]))

        bld.on("vector", [b_rowacc, b_p0], [b_rowacc],
               lambda e: e.scalar_tensor_tensor(out=rowacc[0:1, 0:1024],
                                                in0=p0pool[0:1, 0:1024], scalar=float(TOK),
                                                in1=rowacc[0:1, 0:1024],
                                                op0=ALU.mult, op1=ALU.add))
        bld.on("vector", [b_rowacc, b_p0], [b_rowacc],
               lambda e: e.scalar_tensor_tensor(out=rowacc[0:1, 1024:2048],
                                                in0=p0pool[0:1, 1024:2048], scalar=float(TOK),
                                                in1=rowacc[0:1, 1024:2048],
                                                op0=ALU.mult, op1=ALU.add))

        def build_S(wsl_v, wsl_k, uk_slot, uv_slot, bk_slot, bv_slot, bvs_slot,
                    SXsb, b_SX):
            rotT = [(pp[0], b_pp[0]), (pp[1], b_pp[1]), (ptrf, b_ptrf)]
            ri = 0
            for c1 in range(8):
                for half in range(2):
                    bank, bbank = rotT[ri % 3]
                    ri += 1

                    def tmm(e, c1=c1, half=half, bank=bank, wsl_v=wsl_v):
                        for c2 in range(8):
                            e.matmul(out=bank[:, 0:512],
                                     lhsT=Gsb[:, c2 * C + c1 * 128:c2 * C + (c1 + 1) * 128],
                                     rhs=wslot[wsl_v][:, c2 * C + half * 512:c2 * C + half * 512 + 512],
                                     start=(c2 == 0), stop=False)
                        last = None
                        for k in range(4):
                            ce = half * 4 + k
                            last = e.matmul(out=bank[:, k * 128:(k + 1) * 128],
                                            lhsT=Gsb[:, ce * C + c1 * 128:ce * C + (c1 + 1) * 128],
                                            rhs=ident_bf[:, 0:128],
                                            start=False, stop=(k == 3))
                        return last
                    bld.on("tensor", [b_Gsb, b_wslot[wsl_v], b_ident_bf], [bbank], tmm)
                    bld.on("vector", [bbank], [b_Tsb],
                           lambda e, c1=c1, half=half, bank=bank: e.tensor_copy(
                               out=Tsb[:, c1 * C + half * 512:c1 * C + half * 512 + 512],
                               in_=bank[:, 0:512]))

            def smm2(e, wsl_k=wsl_k, uk_slot=uk_slot, uv_slot=uv_slot,
                     bk_slot=bk_slot, bv_slot=bv_slot, bvs_slot=bvs_slot):
                last = None
                for h in range(16):
                    r, cblk = 64 * (h % 2), 64 * (h // 2)
                    out_ap = pS2[r:r + 64, cblk:cblk + 64]
                    for c in range(8):
                        e.matmul(out=out_ap,
                                 lhsT=wslot[wsl_k][:, c * C + h * 64:c * C + h * 64 + 64],
                                 rhs=Tsb[:, c * C + h * 64:c * C + h * 64 + 64],
                                 start=(c == 0), stop=False)
                    e.matmul(out=out_ap,
                             lhsT=ident_bf[r:r + 64, r:r + 64],
                             rhs=Tsb[r:r + 64, (h // 2) * C + h * 64:(h // 2) * C + h * 64 + 64],
                             start=False, stop=False, tile_position=(r, r))
                    e.matmul(out=out_ap,
                             lhsT=rows2[0:1, uk_slot * C + h * 64:uk_slot * C + h * 64 + 64],
                             rhs=rows2[0:1, bv_slot * C + h * 64:bv_slot * C + h * 64 + 64],
                             start=False, stop=False, tile_position=(0, r))
                    e.matmul(out=out_ap,
                             lhsT=rows2[0:1, bk_slot * C + h * 64:bk_slot * C + h * 64 + 64],
                             rhs=rows2[0:1, uv_slot * C + h * 64:uv_slot * C + h * 64 + 64],
                             start=False, stop=False, tile_position=(0, r))
                    last = e.matmul(out=out_ap,
                                    lhsT=rows2[0:1, bk_slot * C + h * 64:bk_slot * C + h * 64 + 64],
                                    rhs=rows2[0:1, bvs_slot * C + h * 64:bvs_slot * C + h * 64 + 64],
                                    start=False, stop=True, tile_position=(0, r))
                return last
            bld.on("tensor", [b_wslot[wsl_k], b_Tsb, b_ident_bf, b_rows2], [b_pS2], smm2)
            bld.on("vector", [b_pS2], [b_SX],
                   lambda e, SXsb=SXsb: e.tensor_copy(out=SXsb[:], in_=pS2[:, 0:512]))

        build_S(1, 0, 0, 1, 4, 5, 8, S1sb, b_S1sb)
        build_S(3, 2, 2, 3, 6, 7, 9, S2sb, b_S2sb)

        if stage == 1:
            r = _finish_stub(nc, bld, es, y_out, xbuf, b_xbuf, ch_st, ch_x, x_in)
            es_a.close(); es_l0.close()
            return r
        b_ccin = Buf("ccin"); b_ccout = Buf("ccout")
        bld.dma(ch_cc, cc_in[0:128, :], S1sb[:], [b_S1sb], [b_ccin])
        bld.dma(ch_cc, cc_in[128:256, :], S2sb[:], [b_S2sb], [b_ccin])
        bld.dma(ch_cc, cc_in[256:257, :], rowacc[0:1, 0:512], [b_rowacc], [b_ccin])
        bld.dma(ch_cc, cc_in[257:258, :], rowacc[0:1, 512:1024], [b_rowacc], [b_ccin])
        bld.dma(ch_cc, cc_in[258:259, :], rowacc[0:1, 1024:1536], [b_rowacc], [b_ccin])
        bld.dma(ch_cc, cc_in[259:260, :], rowacc[0:1, 1536:2048], [b_rowacc], [b_ccin])

        rg = [[0, 1], [2, 3], [4, 5], [6, 7]]
        stream_g = bld.streams["gpsimd"]
        bld._emit_waits(stream_g, bld._dep_toks([b_ccin], [b_ccout]))

        def cc_item(eng, stream_g=stream_g):
            eng.collective_compute("AllReduce", ALU.add, replica_groups=rg,
                                   ins=[cc_in_t.ap().opt()], outs=[cc_out_t.ap().opt()]
                                   ).then_inc(stream_g.sem, 1)
        stream_g.items.append(("f", cc_item))
        stream_g.count += 1
        bld._track(stream_g, stream_g.count, [b_ccin], [b_ccout])

        S1f, b_S1f = S1sb, b_S1sb
        S2f, b_S2f = S2sb, b_S2sb
        bld.dma(ch_cc, S1f[:], cc_out[0:128, :], [b_ccout], [b_S1f])
        bld.dma(ch_cc, S2f[:], cc_out[128:256, :], [b_ccout], [b_S2f])
        bld.dma(ch_cc, vsum[0:1, 0:512], cc_out[256:257, :], [b_ccout], [b_vsum])
        bld.dma(ch_cc, vsum[0:1, 512:1024], cc_out[257:258, :], [b_ccout], [b_vsum])
        bld.dma(ch_cc, vsum[0:1, 1024:1536], cc_out[258:259, :], [b_ccout], [b_vsum])
        bld.dma(ch_cc, vsum[0:1, 1536:2048], cc_out[259:260, :], [b_ccout], [b_vsum])

        for i in range(1, 5):
            bld.dma(ch_miscf, gsb[0:1, (i - 1) * 16:i * 16],
                    inp[f"g{i}"].unsqueeze(0), [], [b_gsb])
        bld.dma(ch_miscf, gsb[64:65, 16:32], inp["g2"].unsqueeze(0), [], [b_gsb])

        def gpat_src(gi, r):
            return gsb[0:1, (gi - 1) * 16:gi * 16].rearrange(
                "p (c x) -> p c x", x=2)[:, :, r:r + 1].broadcast_to([1, 8, 64])

        def vrow_src(vbase, r):
            return vsum[0:1, vbase:vbase + 1024].rearrange(
                "p (c e) -> p c e", e=64)[:, r:16:2, :]

        def sbf_seg(base, r):
            return srow_bf[0:1, base + r * 512:base + (r + 1) * 512].rearrange(
                "p (c e) -> p c e", e=64)

        def build_vg(e, vbase, gi, dstbase):
            last = None
            for r in range(2):
                last = e.scalar_tensor_tensor(
                    out=sbf_seg(dstbase, r), in0=vrow_src(vbase, r),
                    scalar=1.0 / (NSEQ * NSEQ), in1=gpat_src(gi, r),
                    op0=ALU.mult, op1=ALU.mult)
            return last
        bld.on("vector", [b_vsum, b_gsb], [b_srow_bf], lambda e: build_vg(e, 0, 3, 0))
        bld.on("vector", [b_vsum, b_gsb], [b_srow_bf], lambda e: build_vg(e, 1024, 1, 1024))
        bld.on("vector", [b_gsb], [b_srow_bf],
               lambda e: [e.tensor_single_scalar(out=sbf_seg(2048, r), in_=gpat_src(4, r),
                                                 scalar=1.0 / (NSEQ * NSEQ), op=ALU.mult)
                          for r in range(2)][-1])
        def gpat_src64(gi, r):
            return gsb[64:65, (gi - 1) * 16:gi * 16].rearrange(
                "p (c x) -> p c x", x=2)[:, :, r:r + 1].broadcast_to([1, 8, 64])
        def g2build(e):
            e.tensor_single_scalar(
                out=p0pool[0:1, 0:512].rearrange("p (c e) -> p c e", e=64),
                in_=gpat_src(2, 0), scalar=1.0 / (NSEQ * NSEQ), op=ALU.mult)
            return e.tensor_single_scalar(
                out=p0pool[64:65, 0:512].rearrange("p (c e) -> p c e", e=64),
                in_=gpat_src64(2, 1), scalar=1.0 / (NSEQ * NSEQ), op=ALU.mult)
        bld.on("vector", [b_gsb], [b_p0], g2build)

        def bc(e, dstP, colbase):
            last = None
            for r in range(2):
                last = e.matmul(out=dstP[64 * r:64 * r + 64, 0:512],
                                lhsT=ones_bf[0:1, 0:64],
                                rhs=srow_bf[0:1, colbase + r * 512:colbase + (r + 1) * 512],
                                start=True, stop=True, tile_position=(0, 64 * r))
            return last
        bld.on("tensor", [b_srow_bf, b_ones_bf], [b_pS1], lambda e: bc(e, pS1, 0))
        bld.on("tensor", [b_srow_bf, b_ones_bf], [b_pS2], lambda e: bc(e, pS2, 1024))
        bld.on("vector", [b_S2f, b_pS1], [b_Mtmp],
               lambda e: e.tensor_tensor(out=Mtmp[:], in0=S2f[:], in1=pS1[:, 0:512], op=ALU.mult))
        bld.on("vector", [b_S1f, b_pS2], [b_sqscr],
               lambda e: e.tensor_tensor(out=sqscr[:, 0:512], in0=S1f[:],
                                         in1=pS2[:, 0:512], op=ALU.mult))
        bld.on("vector", [b_Mtmp, b_sqscr], [b_Mtmp],
               lambda e: e.tensor_tensor(out=Mtmp[:], in0=Mtmp[:],
                                         in1=sqscr[:, 0:512], op=ALU.add))
        bld.on("vector", [b_S1f, b_S2f], [b_sqscr],
               lambda e: e.tensor_tensor(out=sqscr[:, 0:512], in0=S1f[:], in1=S2f[:],
                                         op=ALU.mult))
        def y2mm(e):
            e.matmul(out=prs1[0:1, 0:512], lhsT=ones_f[0:64, 0:1],
                     rhs=sqscr[0:64, 0:512], start=True, stop=True)
            return e.matmul(out=prs2[64:65, 0:512], lhsT=ones_f[64:128, 0:1],
                            rhs=sqscr[64:128, 0:512], start=True, stop=True,
                            tile_position=(64, 64))
        bld.on("tensor", [b_sqscr, b_ones_f], [b_prs1, b_prs2], y2mm)
        def y2scale(e):
            e.tensor_tensor(out=y2bf[0:1, 0:512], in0=prs1[0:1, 0:512],
                            in1=p0pool[0:1, 0:512], op=ALU.mult)
            return e.tensor_tensor(out=y2bf[64:65, 0:512], in0=prs2[64:65, 0:512],
                                   in1=p0pool[64:65, 0:512], op=ALU.mult)
        bld.on("vector", [b_prs1, b_prs2, b_p0], [b_y2bf], y2scale)
        bld.on("tensor", [b_srow_bf, b_ones_bf], [b_pS1], lambda e: bc(e, pS1, 2048))
        bld.on("vector", [b_sqscr, b_pS1], [b_sqscr],
               lambda e: e.tensor_tensor(out=sqscr[:, 0:512], in0=sqscr[:, 0:512],
                                         in1=pS1[:, 0:512], op=ALU.mult))
        bld.on("vector", [b_Mtmp, b_sqscr], [b_Mtmp],
               lambda e: e.tensor_tensor(out=Mtmp[:], in0=Mtmp[:],
                                         in1=sqscr[:, 0:512], op=ALU.add))
        bld.on("vector", [b_Mtmp], [b_Mbf],
               lambda e: e.tensor_copy(out=Mbf[:], in_=Mtmp[:]))

        load_wT(0, "Wq")
        load_wT(1, "Wo0")
        load_wT(2, "Wf")

        bld.dma(ch_miscf, frow[0:1, :], inp["bq"].unsqueeze(0), [], [b_frow])
        bld.dma(ch_miscf, frow[32:33, :], inp["ln_g"].unsqueeze(0), [], [b_frow])
        bld.dma(ch_miscf, frow[64:65, :], inp["ln_b"].unsqueeze(0), [], [b_frow])

        def col_tr(e, p, dstcol):
            last = None
            for ci in range(8):
                last = e.transpose(out=ptrf[0:128, dstcol + ci:dstcol + ci + 1],
                                   in_=frow[p:p + 1, ci * 128:(ci + 1) * 128],
                                   identity=ident_f[p:p + 1, p:p + 1])
            return last
        for (p_, dc) in ((0, 0), (32, 8), (64, 16)):
            bld.on("tensor", [b_frow, b_ident_f], [b_ptrf],
                   lambda e, p_=p_, dc=dc: col_tr(e, p_, dc))
            bld.on("vector", [b_ptrf], [b_bcol],
                   lambda e, dc=dc: e.tensor_copy(out=bcol[:, dc:dc + 8],
                                                  in_=ptrf[:, dc:dc + 8]))
        bld.dma(ch_miscf, frow[0:1, :], inp["bo0"].unsqueeze(0), [b_bcol], [b_frow])

        bld.on("vector", [b_bcol], [b_lnbbf],
               lambda e: e.tensor_copy(out=lnbbf[:], in_=bcol[:, 16:24]))

        def bo2mm(e):
            last = None
            for half, bank in ((0, prs1), (1, prs2)):
                for ci in range(8):
                    last = e.matmul(out=bank[0:1, 0:512],
                                    lhsT=lnbbf[:, ci:ci + 1],
                                    rhs=wslot[1][:, ci * C + half * 512:ci * C + half * 512 + 512],
                                    start=(ci == 0), stop=(ci == 7))
            return last
        bld.on("tensor", [b_lnbbf, b_wslot[1]], [b_prs1, b_prs2], bo2mm)

        bld.dma(ch_miscf, p0pool[0:1, 1024:2048], inp["bf"].unsqueeze(0), [b_p0], [b_p0])

        def bias2a(e):
            e.tensor_tensor(out=p0pool[0:1, 0:512], in0=prs1[0:1, 0:512],
                            in1=frow[0:1, 0:512], op=ALU.add)
            return e.tensor_tensor(out=p0pool[0:1, 512:1024], in0=prs2[0:1, 0:512],
                                   in1=frow[0:1, 512:1024], op=ALU.add)
        bld.on("vector", [b_prs1, b_prs2, b_frow, b_p0], [b_p0], bias2a)
        bld.on("vector", [b_p0], [b_brow],
               lambda e: e.tensor_tensor(out=brow_ap("p2bias"), in0=p0pool[0:1, 0:1024],
                                         in1=p0pool[0:1, 1024:2048], op=ALU.add))

        def wo2scale(e):
            last = None
            for ci in range(8):
                last = e.tensor_scalar(out=wslot[1][:, ci * C:(ci + 1) * C],
                                       in0=wslot[1][:, ci * C:(ci + 1) * C],
                                       scalar1=bcol[:, 8 + ci:9 + ci], scalar2=None,
                                       op0=ALU.mult)
            return last
        bld.on("vector", [b_wslot[1], b_bcol], [b_wslot[1]], wo2scale)

        dTt2 = sbT(es_a, "s_dTt2", (128, 1024), BF16); b_dTt2 = Buf("dTt2")
        for ci in range(8):
            bld.dma(ch_w[0], dTt2[:, :], dT_scr[ci * 128:(ci + 1) * 128, :],
                    [b_dTscr], [b_dTt2])
            bld.on("vector", [b_dTt2, b_wslot[2]], [b_wslot[2]],
                   lambda e, ci=ci: e.tensor_tensor(
                       out=wslot[2][:, ci * C:(ci + 1) * C],
                       in0=dTt2[:, :], in1=wslot[2][:, ci * C:(ci + 1) * C], op=ALU.add))

        if debug:
            b_dbg = Buf("dbg")
            bld.dma(ch_st[0], dbg["d_S1"], S1f[:], [b_S1f], [b_dbg])
            bld.dma(ch_st[0], dbg["d_M"], Mtmp[:], [b_Mtmp], [b_dbg])

        bld.barrier()
        es_a.close()
        if stage == 2:
            r = _finish_stub(nc, bld, es, y_out, xbuf, b_xbuf, ch_st, ch_x, x_in)
            es_l0.close()
            return r

        es_b = ExitStack()
        qfm = sbT(es_b, "s_qfm", (128, 8 * 128), BF16); b_qfm = Buf("qfm")
        ylnbf = sbT(es_b, "s_ylnbf", (128, C), BF16); b_ylnbf = Buf("ylnbf")
        ylnfm = sbT(es_b, "s_ylnfm", (128, 8 * 128), BF16); b_ylnfm = Buf("ylnfm")
        xpfm = sbT(es_b, "s_xpfm", (128, 8 * PFX), BF16); b_xpfm = Buf("xpfm")
        x1b = [sbT(es_b, f"s_x1b{i}", (128, C), F32) for i in range(2)]
        b_x1b = [Buf(f"x1b{i}") for i in range(2)]
        fm2 = [sbT(es_b, f"s_fm2_{i}", (128, C), BF16) for i in range(2)]
        b_fm2 = [Buf(f"fm2_{i}") for i in range(2)]

        def p2_fm(t, width, is_prefix):
            if is_prefix:
                return (b_xpfm, lambda ci, w: xpfm[:, ci * PFX:ci * PFX + w])
            return (b_fm2[t % 2],
                    lambda ci, w, t=t: fm2[t % 2][:, ci * 128:ci * 128 + w])

        def fmload(t):
            bi = t % 4
            bld.dma(ch_x[t % 2], xnbf[bi][:, :], xnb_scr[t * 128:(t + 1) * 128, :],
                    [b_xnscr], [b_xnbf[bi]])

        def fmtrans(t):
            bi = t % 4
            xnb, bxnb = xnbf[bi], b_xnbf[bi]

            def fmt(e, xnb=xnb):
                last = None
                for ci in range(8):
                    last = e.transpose(out=ptr[0:128, ci * 128:(ci + 1) * 128],
                                       in_=xnb[:, ci * 128:(ci + 1) * 128],
                                       identity=ident_bf[:, :])
                return last
            bld.on("tensor", [bxnb, b_ident_bf], [b_ptr], fmt)
            bld.on("vector", [b_ptr], [b_fm2[t % 2]],
                   lambda e, t=t: e.tensor_copy(out=fm2[t % 2][:, :],
                                                in_=ptr[0:128, 0:1024]))

        def p2_q(t, width, is_prefix, cos=range(8)):
            b_fm, fm_src = p2_fm(t, width, is_prefix)
            rotb = [(pp[0], b_pp[0]), (pp[1], b_pp[1]), (ptrf, b_ptrf)]
            for co in cos:
                pqb, bpqb = rotb[co % 3]

                def qmm(e, co=co, w=width, fm_src=fm_src, pqb=pqb):
                    last = None
                    for ci in range(8):
                        last = e.matmul(out=pqb[:, 0:w],
                                        lhsT=wslot[0][:, ci * C + co * 128:ci * C + (co + 1) * 128],
                                        rhs=fm_src(ci, w),
                                        start=(ci == 0), stop=(ci == 7))
                    return last
                bld.on("tensor", [b_fm, b_wslot[0]], [bpqb], qmm)

                def qev(e, co=co, w=width, pqb=pqb, fm_src=fm_src):
                    return e.scalar_tensor_tensor(
                        out=qfm[:, co * 128:co * 128 + w],
                        in0=pqb[:, 0:w], scalar=bcol[:, co:co + 1],
                        in1=fm_src(co, w), op0=ALU.add, op1=ALU.add)
                bld.on("vector", [bpqb, b_bcol, b_fm], [b_qfm], qev)

        def p2_restA(t, width, is_prefix):
            bi = t % 2
            xb, bxb = xbuf[bi], b_xbuf[bi]
            b_fm, fm_src = p2_fm(t, width, is_prefix)
            if not is_prefix:
                bld.dma(ch_x[bi], xb[:, :], x_in[t * 128:(t + 1) * 128, :], [], [bxb])

            def ymm(e, w=width):
                last = None
                for h in range(16):
                    r, cblk = h % 2, h // 2
                    bank = (pS1, pS2)[r]
                    coff = cblk * 64
                    e.matmul(out=bank[0:w, coff:coff + 64],
                             lhsT=qfm[64 * r:64 * r + 64, cblk * 128:cblk * 128 + w],
                             rhs=Mbf[64 * r:64 * r + 64, cblk * 64:cblk * 64 + 64],
                             start=True, stop=False)
                    last = e.matmul(out=bank[0:w, coff:coff + 64],
                                    lhsT=ones_bf[64 * r:64 * r + 1, 0:w],
                                    rhs=y2bf[64 * r:64 * r + 1, cblk * 64:cblk * 64 + 64],
                                    start=False, stop=True)
                return last
            bld.on("tensor", [b_qfm, b_Mbf, b_y2bf, b_ones_bf], [b_pS1, b_pS2], ymm)

        def p2_restLN(t, width, is_prefix):
            bi = t % 2

            b_rdA, b_rdB = Buf("rdA"), Buf("rdB")
            bld.on("scalar", [b_pS1], [b_sqscr, b_lnA, b_rdA],
                   lambda e, w=width: e.activation(
                       out=sqscr[0:w, 0:512], in_=pS1[0:w, 0:512], func=AF.Square,
                       accum_out=lnsq[0:w, 1:2]))
            bld.on("vector", [b_pS2], [b_lnst, b_rdB],
                   lambda e, w=width: e.reduce_sum(
                       out=lnst[0:w, 11:12], in_=pS2[0:w, 0:512], axis=AX))
            bld.on("scalar", [b_pS2, b_rdB], [b_sqscr, b_lnA],
                   lambda e, w=width: e.activation(
                       out=sqscr[0:w, 512:1024], in_=pS2[0:w, 0:512], func=AF.Square,
                       accum_out=lnsq[0:w, 2:3]))
            bld.on("vector", [b_pS1, b_rdA], [b_lnst],
                   lambda e, w=width: e.reduce_sum(
                       out=lnst[0:w, 10:11], in_=pS1[0:w, 0:512], axis=AX))

            bld.on("vector", [b_lnst], [b_lnst],
                   lambda e, w=width: e.tensor_tensor(
                       out=lnst[0:w, 10:11], in0=lnst[0:w, 10:11],
                       in1=lnst[0:w, 11:12], op=ALU.add))
            bld.on("vector", [b_lnst], [b_lnst],
                   lambda e, w=width: e.tensor_single_scalar(
                       out=lnst[0:w, 12:13], in_=lnst[0:w, 10:11],
                       scalar=1.0 / C, op=ALU.mult))
            bld.on("vector", [b_lnst], [b_lnst],
                   lambda e, w=width: e.tensor_scalar(
                       out=lnst[0:w, 13:14], in0=lnst[0:w, 10:11],
                       scalar1=lnst[0:w, 10:11], scalar2=1.0 / (C * C),
                       op0=ALU.mult, op1=ALU.mult))
            bld.on("vector", [b_lnA], [b_lnA],
                   lambda e, w=width: e.tensor_tensor(
                       out=lnsq[0:w, 1:2], in0=lnsq[0:w, 1:2],
                       in1=lnsq[0:w, 2:3], op=ALU.add))
            bld.on("vector", [b_lnst, b_lnA], [b_lnst],
                   lambda e, w=width: e.scalar_tensor_tensor(
                       out=lnst[0:w, 14:15], in0=lnsq[0:w, 1:2],
                       scalar=1.0 / C, in1=lnst[0:w, 13:14],
                       op0=ALU.mult, op1=ALU.subtract))

            bld.on("scalar", [b_lnst], [b_lnst],
                   lambda e, w=width: e.activation(out=lnst[0:w, 15:16],
                                                   in_=lnst[0:w, 14:15], func=AF.Ln,
                                                   bias=kvec[0:w, 0:1]))
            bld.on("scalar", [b_lnst], [b_lnst],
                   lambda e, w=width: e.activation(out=lnst[0:w, 14:15],
                                                   in_=lnst[0:w, 15:16],
                                                   func=AF.Exp, scale=-0.5))

            def yapply(e, w=width):
                dst0 = ylnbf[0:w, :].rearrange("p (h x) -> p h x", x=64)[:, 0:16:2, :]
                dst1 = ylnbf[0:w, :].rearrange("p (h x) -> p h x", x=64)[:, 1:16:2, :]
                e.tensor_scalar(out=dst0, in0=pS1[0:w, 0:512],
                                scalar1=lnst[0:w, 12:13], scalar2=lnst[0:w, 14:15],
                                op0=ALU.subtract, op1=ALU.mult)
                return e.tensor_scalar(out=dst1, in0=pS2[0:w, 0:512],
                                       scalar1=lnst[0:w, 12:13], scalar2=lnst[0:w, 14:15],
                                       op0=ALU.subtract, op1=ALU.mult)
            bld.on("vector", [b_pS1, b_pS2, b_lnst], [b_ylnbf], yapply)

        def p2_restB(t, width, is_prefix):
            bi = t % 2
            xb, bxb = xbuf[bi], b_xbuf[bi]
            b_fm, fm_src = p2_fm(t, width, is_prefix)
            fmdst2 = ylnfm[:, :].rearrange("p (ci t) -> p ci t", ci=8)[:, :, 0:width]
            transpose_tile_to_fm(ylnbf, b_ylnbf, fmdst2, b_ylnfm, width)

            for half, pob, bpob in ((0, prs1, b_prs1), (1, prs2, b_prs2)):
                def offmm(e, half=half, w=width, fm_src=fm_src, pob=pob):
                    for ci in range(8):
                        e.matmul(out=pob[0:w, 0:512],
                                 lhsT=ylnfm[:, ci * 128:ci * 128 + w],
                                 rhs=wslot[1][:, ci * C + half * 512:ci * C + half * 512 + 512],
                                 start=(ci == 0), stop=False)
                    for ci in range(8):
                        e.matmul(out=pob[0:w, 0:512],
                                 lhsT=fm_src(ci, w),
                                 rhs=wslot[2][:, ci * C + half * 512:ci * C + half * 512 + 512],
                                 start=False, stop=False)
                    bp = BRP["p2bias"][0]
                    return e.matmul(out=pob[0:w, 0:512],
                                    lhsT=ones_bf[bp:bp + 1, 0:w],
                                    rhs=brow_ap("p2bias", half * 512, 512),
                                    start=False, stop=True)
                bld.on("tensor", [b_ylnfm, b_fm, b_wslot[1], b_wslot[2], b_ones_bf, b_brow],
                       [bpob], offmm)

            if is_prefix:
                def x1ev(e, w=width, xb=xb):
                    e.tensor_tensor(out=x1p[0:w, 0:512], in0=prs1[0:w, 0:512],
                                    in1=xb[0:w, 0:512], op=ALU.add)
                    return e.tensor_tensor(out=x1p[0:w, 512:1024], in0=prs2[0:w, 0:512],
                                           in1=xb[0:w, 512:1024], op=ALU.add)
                bld.on("vector", [b_prs1, b_prs2, bxb], [b_x1p], x1ev)
            else:
                xo, bxo = x1b[bi], b_x1b[bi]

                def x1ev(e, w=width, xo=xo, xb=xb):
                    e.tensor_tensor(out=xo[0:w, 0:512], in0=prs1[0:w, 0:512],
                                    in1=xb[0:w, 0:512], op=ALU.add)
                    return e.tensor_tensor(out=xo[0:w, 512:1024], in0=prs2[0:w, 0:512],
                                           in1=xb[0:w, 512:1024], op=ALU.add)
                bld.on("vector", [b_prs1, b_prs2, bxb], [bxo], x1ev)
                bld.dma(ch_st[bi], x1_scr[t * 128:t * 128 + width, :], xo[0:width, :],
                        [bxo], [Buf(f"x1scr_t{t}")])

        xnbp, bxnbp = xnbf[0], b_xnbf[0]
        ln_tile(xp_in[0:PFX, :], PFX, xbuf[0], b_xbuf[0], xnbp, bxnbp, ch_x[0])
        fmdstp2 = xpfm[:, :].rearrange("p (ci t) -> p ci t", ci=8)[:, :, 0:PFX]
        transpose_tile_to_fm(xnbp, bxnbp, fmdstp2, b_xpfm, PFX)
        p2_q(0, PFX, True)
        p2_restA(0, PFX, True)
        p2_restLN(0, PFX, True)
        p2_restB(0, PFX, True)
        fmload(0)
        fmload(1)
        fmtrans(0)
        p2_q(0, 128, False)
        for t in range(NT):
            if t + 2 < NT:
                fmload(t + 2)
            if t + 1 < NT:
                fmtrans(t + 1)
            p2_restA(t, 128, False)
            if t + 1 < NT:
                p2_q(t + 1, 128, False, cos=range(0, 3))
                p2_restLN(t, 128, False)
                p2_q(t + 1, 128, False, cos=range(3, 8))
            else:
                p2_restLN(t, 128, False)
            p2_restB(t, 128, False)

        bld.barrier()
        es_b.close()
        es_l0.close()

        if debug:
            b_dbg2 = Buf("dbg2")
            xtmp, bxt = xbuf[0], b_xbuf[0]
            for t in range(NT):
                bld.dma(ch_x[0], xtmp[:, :], x1_scr[t * 128:(t + 1) * 128, :], [bxt], [bxt])
                bld.dma(ch_st[0], dbg["d_x1"][t * 128:(t + 1) * 128, :], xtmp[:, :],
                        [bxt], [b_dbg2])

        if stage == 3:
            return _finish_stub(nc, bld, es, y_out, xbuf, b_xbuf, ch_st, ch_x, x_in,
                                src_scr=x1_scr)
        es_c = ExitStack()
        xn1g = sbT(es_c, "s_xn1g", (128, 8 * GL), BF16); b_xn1g = Buf("xn1g")
        qg = sbT(es_c, "s_qg", (128, 8 * GL), BF16); b_qg = Buf("qg")
        kg_ = sbT(es_c, "s_kg", (128, 8 * GL), BF16); b_kg = Buf("kg")
        v65 = sbT(es_c, "s_v65", (128, (GL // 128) * 16 * 65), BF16); b_v65 = Buf("v65")
        attg = sbT(es_c, "s_attg", (128, 8 * GL), BF16); b_attg = Buf("attg")
        accg = sbT(es_c, "s_accg", (128, (GL // 128) * C), F32); b_accg = Buf("accg")
        berow = sbT(es_c, "s_berow", (128, 3 * C), BF16); b_berow = Buf("berow")
        xp1fm = sbT(es_c, "s_xp1fm", (128, 8 * PFX), BF16); b_xp1fm = Buf("xp1fm")
        kgfm = sbT(es_c, "s_kgfm", (128, 8 * PFX), BF16); b_kgfm = Buf("kgfm")
        vg65 = sbT(es_c, "s_vg65", (64, 16 * 65), BF16); b_vg65 = Buf("vg65")
        pTl = sbT(es_c, "s_pTl", (128, 8 * 128), BF16); b_pTl = Buf("pTl")
        pTg = sbT(es_c, "s_pTg", (64, 8 * 128), BF16); b_pTg = Buf("pTg")
        zrec = sbT(es_c, "s_zrec", (128, 32), F32); b_zrec = Buf("zrec")
        atfm = sbT(es_c, "s_atfm", (128, 8 * 128), BF16); b_atfm = Buf("atfm")
        gmax = sbT(es_c, "s_gmax", (128, 16), F32); b_gmax = Buf("gmax")
        gm8 = sbT(es_c, "s_gm8", (128, 24), F32); b_gm8 = Buf("gm8")
        gidx = sbT(es_c, "s_gidx", (128, 24), U32); b_gidx = Buf("gidx")
        maskp = sbT(es_c, "s_maskp", (128, (GL // 128) * 8), F32); b_maskp = Buf("maskp")
        expv = sbT(es_c, "s_expv", (128, C), BF16); b_expv = Buf("expv")
        tmpe = sbT(es_c, "s_tmpe", (128, C), F32); b_tmpe = Buf("tmpe")
        tmpe2 = sbT(es_c, "s_tmpe2", (128, C), F32); b_tmpe2 = Buf("tmpe2")
        x2b = [sbT(es_c, f"s_x2b{i}", (128, C), F32) for i in range(2)]
        xbufH = sbT(es_c, "s_xbufH", (128, C), F32); b_xbufH = Buf("xbufH")
        xnbfH = sbT(es_c, "s_xnbfH", (128, C), BF16); b_xnbfH = Buf("xnbfH")
        b_x2b = [Buf(f"x2b{i}") for i in range(2)]

        BE = {}
        _beslots = [(p, c) for c in range(3) for p in (0, 32, 64)]
        for e_ in range(8):
            BE[e_] = _beslots[e_]
            p, cslot = BE[e_]
            bld.dma(ch_misc, berow[p:p + 1, cslot * C:(cslot + 1) * C],
                    inp["be"][e_:e_ + 1, :], [], [b_berow])

        def be_ap(e_, lo, ln):
            p, cslot = BE[e_]
            return berow[p:p + 1, cslot * C + lo:cslot * C + lo + ln]

        load_wT(0, "W1q")
        load_wT(1, "W1k")
        load_wT(2, "W1v")

        xnb, bxnb = xnbf[1], b_xnbf[1]
        bld.on("scalar", [b_x1p], [b_sqscr, b_lnst],
               lambda e: e.activation(out=sqscr[0:PFX, :], in_=x1p[0:PFX, :],
                                      func=AF.Square, accum_out=lnst[0:PFX, 1:2]))
        bld.on("vector", [b_x1p], [b_lnst],
               lambda e: e.reduce_sum(out=lnst[0:PFX, 0:1], in_=x1p[0:PFX, :], axis=AX))
        bld.on("vector", [b_lnst], [b_lnst],
               lambda e: e.tensor_single_scalar(out=lnst[0:PFX, 2:3], in_=lnst[0:PFX, 0:1],
                                                scalar=1.0 / C, op=ALU.mult))
        bld.on("vector", [b_lnst], [b_lnst],
               lambda e: e.tensor_tensor(out=lnst[0:PFX, 3:4], in0=lnst[0:PFX, 2:3],
                                         in1=lnst[0:PFX, 2:3], op=ALU.mult))
        bld.on("vector", [b_lnst], [b_lnst],
               lambda e: e.scalar_tensor_tensor(out=lnst[0:PFX, 4:5], in0=lnst[0:PFX, 1:2],
                                                scalar=1.0 / C, in1=lnst[0:PFX, 3:4],
                                                op0=ALU.mult, op1=ALU.subtract))
        bld.on("scalar", [b_lnst], [b_lnst],
               lambda e: e.activation(out=lnst[0:PFX, 5:6], in_=lnst[0:PFX, 4:5],
                                      func=AF.Ln, bias=kvec[0:PFX, 0:1]))
        bld.on("scalar", [b_lnst], [b_lnst],
               lambda e: e.activation(out=lnst[0:PFX, 6:7], in_=lnst[0:PFX, 5:6],
                                      func=AF.Exp, scale=-0.5))
        bld.on("vector", [b_x1p, b_lnst], [bxnb],
               lambda e: e.tensor_scalar(out=xnb[0:PFX, :], in0=x1p[0:PFX, :],
                                         scalar1=lnst[0:PFX, 2:3], scalar2=lnst[0:PFX, 6:7],
                                         op0=ALU.subtract, op1=ALU.mult))
        fmdstp = xp1fm[:, :].rearrange("p (ci t) -> p ci t", ci=8)[:, :, 0:PFX]
        transpose_tile_to_fm(xnb, bxnb, fmdstp, b_xp1fm, PFX)
        ln_tile(x1_scr[0:128, :], 128, xbufH, b_xbufH, xnbfH, b_xnbfH, ch_x[0])

        for co in range(8):
            pb, bpb = pp[co % 2], b_pp[co % 2]

            def kgmm(e, co=co, pb=pb):
                last = None
                for ci in range(8):
                    last = e.matmul(out=pb[:, 0:PFX],
                                    lhsT=wslot[1][:, ci * C + co * 128:ci * C + (co + 1) * 128],
                                    rhs=xp1fm[:, ci * PFX:(ci + 1) * PFX],
                                    start=(ci == 0), stop=(ci == 7))
                return last
            bld.on("tensor", [b_xp1fm, b_wslot[1]], [bpb], kgmm)
            bld.on("vector", [bpb], [b_kgfm],
                   lambda e, co=co, pb=pb: e.tensor_copy(
                       out=kgfm[:, co * PFX:(co + 1) * PFX], in_=pb[:, 0:PFX]))

        for half, pob, bpob in ((0, prs1, b_prs1), (1, prs2, b_prs2)):
            def vgmm(e, half=half, pob=pob):
                last = None
                for ci in range(8):
                    last = e.matmul(out=pob[0:PFX, 0:512],
                                    lhsT=xp1fm[:, ci * PFX:(ci + 1) * PFX],
                                    rhs=wslot[2][:, ci * C + half * 512:ci * C + half * 512 + 512],
                                    start=(ci == 0), stop=(ci == 7))
                return last
            bld.on("tensor", [b_xp1fm, b_wslot[2]], [bpob], vgmm)
        bld.on("vector", [b_ident_f], [b_vg65],
               lambda e: e.memset(vg65[0:64, :], 0.0))
        bld.on("vector", [b_vg65], [b_vg65],
               lambda e: e.memset(
                   vg65[0:64, :].rearrange("p (h x) -> p h x", x=65)[:, :, 64:65], 1.0))
        bld.on("vector", [b_prs1, b_vg65], [b_vg65],
               lambda e: e.tensor_copy(
                   out=vg65[0:64, :].rearrange("p (h x) -> p h x", x=65)[:, 0:8, 0:64],
                   in_=prs1[0:PFX, 0:512].rearrange("p (h d) -> p h d", d=64)))
        bld.on("vector", [b_prs2, b_vg65], [b_vg65],
               lambda e: e.tensor_copy(
                   out=vg65[0:64, :].rearrange("p (h x) -> p h x", x=65)[:, 8:16, 0:64],
                   in_=prs2[0:PFX, 0:512].rearrange("p (h d) -> p h d", d=64)))

        NSUB = GL // 128

        for g in range(NG):
            t0 = g * NSUB
            for st_ in range(NSUB):
                t = t0 + st_
                bi = t % 2
                if st_ == 0:
                    xnb2, bxnb2 = xnbfH, b_xnbfH
                else:
                    xnb2, bxnb2 = xnbf[bi], b_xnbf[bi]
                if st_ + 1 < NSUB:
                    bj = (t + 1) % 2
                    ln_tile(x1_scr[(t + 1) * 128:(t + 2) * 128, :], 128, xbuf[bj],
                            b_xbuf[bj], xnbf[bj], b_xnbf[bj], ch_x[bj])
                fmdst = xn1g[:, :].rearrange("p (ci t) -> p ci t", ci=8)[
                    :, :, st_ * 128:(st_ + 1) * 128]
                transpose_tile_to_fm(xnb2, bxnb2, fmdst, b_xn1g)

            rotb = [(pp[0], b_pp[0]), (pp[1], b_pp[1]), (ptrf, b_ptrf)]
            rot_i = 0
            for wslotid, dstt, b_dst in ((0, qg, b_qg), (1, kg_, b_kg)):
                for co in range(8):
                    pb, bpb = rotb[rot_i % 3]
                    rot_i += 1

                    def pmm(e, co=co, pb=pb, wslotid=wslotid):
                        last = None
                        for ci in range(8):
                            last = e.matmul(
                                out=pb[:, 0:GL],
                                lhsT=wslot[wslotid][:, ci * C + co * 128:ci * C + (co + 1) * 128],
                                rhs=xn1g[:, ci * GL:(ci + 1) * GL],
                                start=(ci == 0), stop=(ci == 7))
                        return last
                    bld.on("tensor", [b_xn1g, b_wslot[wslotid]], [bpb], pmm)
                    bld.on("vector", [bpb], [b_dst],
                           lambda e, co=co, pb=pb, dstt=dstt: e.tensor_copy(
                               out=dstt[:, co * GL:(co + 1) * GL], in_=pb[:, 0:GL]))

            if g == 0:
                bld.on("vector", [], [b_v65], lambda e: e.memset(v65[:, :], 0.0))
                bld.on("vector", [b_v65], [b_v65],
                       lambda e: e.memset(
                           v65[:, :].rearrange("p (s h x) -> p s h x", s=NSUB, x=65)[:, :, :, 64:65], 1.0))
            for st_ in range(NSUB):
                for half, pob, bpob in ((0, prs1, b_prs1), (1, prs2, b_prs2)):
                    def vmm(e, st_=st_, half=half, pob=pob):
                        last = None
                        for ci in range(8):
                            last = e.matmul(
                                out=pob[0:128, 0:512],
                                lhsT=xn1g[:, ci * GL + st_ * 128:ci * GL + (st_ + 1) * 128],
                                rhs=wslot[2][:, ci * C + half * 512:ci * C + half * 512 + 512],
                                start=(ci == 0), stop=(ci == 7))
                        return last
                    bld.on("tensor", [b_xn1g, b_wslot[2]], [bpob], vmm)
                    bld.on("vector", [bpob], [b_v65],
                           lambda e, st_=st_, half=half, pob=pob: e.tensor_copy(
                               out=v65[:, st_ * 16 * 65:(st_ + 1) * 16 * 65].rearrange(
                                   "p (h x) -> p h x", x=65)[:, half * 8:(half + 1) * 8, 0:64],
                               in_=pob[0:128, 0:512].rearrange("p (h d) -> p h d", d=64)))

            for st_ in (range(NSUB) if l1sub >= 2 else []):
                vloc = v65[:, st_ * 16 * 65:(st_ + 1) * 16 * 65]
                for hg in range(2):
                    def stmm(e, st_=st_, hg=hg):
                        last = None
                        for hh in range(8):
                            h = hg * 8 + hh
                            r, cblk = h % 2, h // 2
                            bank = pp[hh % 2]
                            coff = (hh // 2) * 128
                            last = e.matmul(
                                out=bank[:, coff:coff + 128],
                                lhsT=kg_[64 * r:64 * r + 64,
                                         cblk * GL + st_ * 128:cblk * GL + (st_ + 1) * 128],
                                rhs=qg[64 * r:64 * r + 64,
                                       cblk * GL + st_ * 128:cblk * GL + (st_ + 1) * 128],
                                start=True, stop=True)
                        return last
                    bld.on("tensor", [b_qg, b_kg], [b_pp[0], b_pp[1]], stmm)
                    if attnparts < 1:
                        continue

                    def sgmm(e, st_=st_, hg=hg):
                        last = None
                        for hh in range(8):
                            h = hg * 8 + hh
                            r, cblk = h % 2, h // 2
                            bank = (pS1, pS2)[hh % 2]
                            coff = (hh // 2) * 128
                            last = e.matmul(
                                out=bank[0:64, coff:coff + 128],
                                lhsT=kgfm[64 * r:64 * r + 64, cblk * PFX:(cblk + 1) * PFX],
                                rhs=qg[64 * r:64 * r + 64,
                                       cblk * GL + st_ * 128:cblk * GL + (st_ + 1) * 128],
                                start=True, stop=True)
                        return last
                    bld.on("tensor", [b_qg, b_kgfm], [b_pS1, b_pS2], sgmm)
                    if attnparts < 2:
                        continue

                    def expl(e, hg=hg):
                        e.activation(out=pTl[:, 0:512], in_=pp[0][:, 0:512], func=AF.Exp,
                                     scale=0.125, bias=kvec[:, 1:2])
                        return e.activation(out=pTl[:, 512:1024], in_=pp[1][:, 0:512],
                                            func=AF.Exp, scale=0.125, bias=kvec[:, 1:2])
                    bld.on("scalar", [b_pp[0], b_pp[1]], [b_pTl], expl)

                    def expg(e, hg=hg):
                        e.activation(out=pTg[0:64, 0:512], in_=pS1[0:64, 0:512], func=AF.Exp,
                                     scale=0.125, bias=kvec[0:64, 1:2])
                        return e.activation(out=pTg[0:64, 512:1024], in_=pS2[0:64, 0:512],
                                            func=AF.Exp, scale=0.125, bias=kvec[0:64, 1:2])
                    bld.on("scalar", [b_pS1, b_pS2], [b_pTg], expg)
                    if attnparts < 3:
                        continue

                    def avmm(e, st_=st_, hg=hg, vloc=vloc):
                        last = None
                        for hh in range(8):
                            h = hg * 8 + hh
                            bank = (prs1, prs2)[hh // 4]
                            coff = (hh % 4) * 65
                            pcol = (hh % 2) * 512 + (hh // 2) * 128
                            e.matmul(out=bank[:, coff:coff + 65],
                                     lhsT=pTl[:, pcol:pcol + 128],
                                     rhs=vloc[:, h * 65:(h + 1) * 65],
                                     start=True, stop=False)
                            last = e.matmul(out=bank[:, coff:coff + 65],
                                            lhsT=pTg[0:64, pcol:pcol + 128],
                                            rhs=vg65[0:64, h * 65:(h + 1) * 65],
                                            start=False, stop=True)
                        return last
                    bld.on("tensor", [b_pTl, b_pTg, b_v65, b_vg65],
                           [b_prs1, b_prs2], avmm)
                    if attnparts < 4:
                        continue

                    def normZ(e, st_=st_, hg=hg):
                        last = None
                        for bk, bank in ((0, prs1), (1, prs2)):
                            last = e.reciprocal(
                                out=zrec[:, hg * 8 + bk * 4:hg * 8 + bk * 4 + 4],
                                in_=bank[:, 0:260].rearrange("p (h x) -> p h x", x=65)[:, :, 64:65].rearrange("p h x -> p (h x)"))
                        return last
                    bld.on("vector", [b_prs1, b_prs2], [b_zrec], normZ)
                    def normM(e, st_=st_, hg=hg):
                        last = None
                        for bk, bank in ((0, prs1), (1, prs2)):
                            last = e.tensor_tensor(
                                out=attg[:, st_ * C + hg * 512 + bk * 256:
                                         st_ * C + hg * 512 + (bk + 1) * 256].rearrange(
                                             "p (h d) -> p h d", d=64),
                                in0=bank[:, 0:260].rearrange("p (h x) -> p h x", x=65)[:, :, 0:64],
                                in1=zrec[:, hg * 8 + bk * 4:hg * 8 + bk * 4 + 4].unsqueeze(2).broadcast_to(
                                    [128, 4, 64]),
                                op=ALU.mult)
                        return last
                    bld.on("vector", [b_prs1, b_prs2, b_zrec], [b_attg], normM)

            if l1sub >= 5:
                load_wT(1, "W1o", chi=g)
            if l1sub >= 3:
                load_wN(3, "Wg", chi=g)
            for st_ in (range(NSUB) if l1sub >= 3 else []):
                gbanks = (pp[0], pp[1]) if st_ % 2 == 0 else (pS1, pS2)
                gbufs = (b_pp[0], b_pp[1]) if st_ % 2 == 0 else (b_pS1, b_pS2)
                def gmm(e, st_=st_, gbanks=gbanks):
                    last = None
                    for half, bank in ((0, gbanks[0]), (1, gbanks[1])):
                        for ci in range(8):
                            e.matmul(out=bank[:, 0:512],
                                     lhsT=xn1g[:, ci * GL + st_ * 128:ci * GL + (st_ + 1) * 128],
                                     rhs=wslot[3][:, ci * C + half * 512:ci * C + half * 512 + 512],
                                     start=(ci == 0), stop=False)
                        bp = BRP["bg"][0]
                        last = e.matmul(out=bank[:, 0:512],
                                        lhsT=ones_bf[bp:bp + 1, 0:128],
                                        rhs=brow_ap("bg", half * 512, 512),
                                        start=False, stop=True)
                    return last
                bld.on("tensor", [b_xn1g, b_wslot[3], b_ones_bf, b_brow],
                       [gbufs[0], gbufs[1]], gmm)

                def gstats1(e, st_=st_, gbanks=gbanks):
                    e.reduce_max(out=gmax[:, 0:1], in_=gbanks[0][:, 0:512], axis=AX)
                    return e.reduce_max(out=gmax[:, 1:2], in_=gbanks[1][:, 0:512], axis=AX)
                bld.on("vector", [gbufs[0], gbufs[1]], [b_gmax], gstats1)
                bld.on("vector", [b_gmax], [b_gmax],
                       lambda e: e.tensor_tensor(out=gmax[:, 2:3], in0=gmax[:, 0:1],
                                                 in1=gmax[:, 1:2], op=ALU.max))
                bld.on("vector", [b_gmax], [b_gmax],
                       lambda e: e.tensor_single_scalar(out=gmax[:, 3:4], in_=gmax[:, 2:3],
                                                        scalar=-1.0, op=ALU.mult))

                def gexp(e, st_=st_, gbanks=gbanks):
                    e.activation(out=expv[:, 0:512], in_=gbanks[0][:, 0:512], func=AF.Exp,
                                 bias=gmax[:, 3:4], accum_out=gmax[:, 4:5])
                    return e.activation(out=expv[:, 512:1024], in_=gbanks[1][:, 0:512],
                                        func=AF.Exp, bias=gmax[:, 3:4], accum_out=gmax[:, 5:6])
                bld.on("scalar", [gbufs[0], gbufs[1], b_gmax], [b_expv, b_gmax], gexp)

                bld.on("vector", [b_gmax], [b_gmax],
                       lambda e: e.tensor_tensor(out=gmax[:, 6:7], in0=gmax[:, 4:5],
                                                 in1=gmax[:, 5:6], op=ALU.add))
                bld.on("vector", [b_gmax], [b_gmax],
                       lambda e: e.reciprocal(out=gmax[:, 7:8], in_=gmax[:, 6:7]))

                def gidxB2(e, st_=st_, gbanks=gbanks):
                    e.max_index(out=gidx[:, 0:8], in_max=gm8[:, 8:16],
                                in_values=gbanks[0][:, 0:512])
                    return e.max_index(out=gidx[:, 8:16], in_max=gm8[:, 16:24],
                                       in_values=gbanks[1][:, 0:512])
                def gidxA2(e, st_=st_):
                    e.tensor_copy(out=gm8[:, 8:16], in_=gmax[:, 0:1].broadcast_to([128, 8]))
                    return e.tensor_copy(out=gm8[:, 16:24],
                                         in_=gmax[:, 1:2].broadcast_to([128, 8]))
                bld.on("vector", [b_gmax], [b_gm8], gidxA2)
                bld.on("vector", [gbufs[0], gbufs[1], b_gm8], [b_gidx], gidxB2)
                def gidxC(e, st_=st_):
                    e.tensor_copy(out=gmax[:, 8:9], in_=gidx[:, 0:1])
                    return e.tensor_copy(out=gmax[:, 9:10], in_=gidx[:, 8:9])
                bld.on("vector", [b_gidx], [b_gmax], gidxC)
                bld.on("vector", [b_gmax], [b_gmax],
                       lambda e: e.tensor_tensor(out=gmax[:, 12:13], in0=gmax[:, 1:2],
                                                 in1=gmax[:, 0:1], op=ALU.is_gt))
                bld.on("vector", [b_gmax], [b_gmax],
                       lambda e: e.tensor_tensor(out=gmax[:, 13:14], in0=gmax[:, 9:10],
                                                 in1=gmax[:, 8:9], op=ALU.subtract))
                bld.on("vector", [b_gmax], [b_gmax],
                       lambda e: e.scalar_tensor_tensor(out=gmax[:, 10:11],
                                                        in0=gmax[:, 12:13], scalar=1.0,
                                                        in1=gmax[:, 13:14],
                                                        op0=ALU.mult, op1=ALU.mult))
                bld.on("vector", [b_gmax], [b_gmax],
                       lambda e: e.tensor_tensor(out=gmax[:, 10:11], in0=gmax[:, 10:11],
                                                 in1=gmax[:, 8:9], op=ALU.add))
                bld.on("vector", [b_gmax], [b_gidx],
                       lambda e: e.tensor_copy(out=gidx[:, 16:17], in_=gmax[:, 10:11]))
                bld.on("vector", [b_gidx], [b_gidx],
                       lambda e: e.tensor_single_scalar(out=gidx[:, 17:18], in_=gidx[:, 16:17],
                                                        scalar=7, op=ALU.bitwise_and))
                bld.on("vector", [b_gidx], [b_gmax],
                       lambda e: e.tensor_copy(out=gmax[:, 11:12], in_=gidx[:, 17:18]))

                bld.on("vector", [b_gmax, b_iota8], [b_maskp],
                       lambda e, st_=st_: e.tensor_tensor(
                           out=maskp[:, st_ * 8:(st_ + 1) * 8],
                           in0=gmax[:, 11:12].broadcast_to([128, 8]),
                           in1=iota8[:, 0:8], op=ALU.is_equal))
                bld.on("vector", [b_gmax, b_maskp], [b_maskp],
                       lambda e, st_=st_: e.tensor_scalar(
                           out=maskp[:, st_ * 8:(st_ + 1) * 8],
                           in0=maskp[:, st_ * 8:(st_ + 1) * 8],
                           scalar1=gmax[:, 7:8], scalar2=None, op0=ALU.mult))

            if g + 1 < NG:
                tn = (g + 1) * NSUB
                ln_tile(x1_scr[tn * 128:(tn + 1) * 128, :], 128, xbufH, b_xbufH,
                        xnbfH, b_xnbfH, ch_x[0])
            for e_ in (range(8) if l1sub >= 4 else []):
                wsl = 3 if e_ % 2 == 0 else 0
                load_wN(wsl, f"We{e_}", chi=e_)
                for st_ in range(NSUB):
                    t = t0 + st_
                    bi = st_ % 2
                    ebanks = (pp[0], pp[1]) if st_ % 2 == 0 else (pS1, pS2)
                    ebufs = (b_pp[0], b_pp[1]) if st_ % 2 == 0 else (b_pS1, b_pS2)
                    etmp, betmp = (tmpe, b_tmpe) if st_ % 2 == 0 else (tmpe2, b_tmpe2)
                    def hmm(e, st_=st_, wsl=wsl, e_=e_, ebanks=ebanks):
                        last = None
                        for half, bank in ((0, ebanks[0]), (1, ebanks[1])):
                            for ci in range(8):
                                e.matmul(out=bank[:, 0:512],
                                         lhsT=xn1g[:, ci * GL + st_ * 128:ci * GL + (st_ + 1) * 128],
                                         rhs=wslot[wsl][:, ci * C + half * 512:ci * C + half * 512 + 512],
                                         start=(ci == 0), stop=False)
                            bp = BE[e_][0]
                            last = e.matmul(out=bank[:, 0:512],
                                            lhsT=ones_bf[bp:bp + 1, 0:128],
                                            rhs=be_ap(e_, half * 512, 512),
                                            start=False, stop=True)
                        return last
                    bld.on("tensor", [b_xn1g, b_wslot[wsl], b_ones_bf, b_berow],
                           [ebufs[0], ebufs[1]], hmm)

                    def sel(e, st_=st_, e_=e_, ebanks=ebanks, etmp=etmp):
                        mp = maskp[:, st_ * 8 + e_:st_ * 8 + e_ + 1].broadcast_to([128, 512])
                        e.scalar_tensor_tensor(out=etmp[:, 0:512], in0=ebanks[0][:, 0:512],
                                               scalar=0.0, in1=mp,
                                               op0=ALU.max, op1=ALU.mult)
                        return e.scalar_tensor_tensor(out=etmp[:, 512:1024], in0=ebanks[1][:, 0:512],
                                                      scalar=0.0, in1=mp,
                                                      op0=ALU.max, op1=ALU.mult)
                    bld.on("vector", [ebufs[0], ebufs[1], b_maskp], [betmp], sel)

                    if e_ == 0:
                        xb, bxb = xbuf[bi], b_xbuf[bi]
                        bld.dma(ch_x[bi], xb[:, :], x1_scr[t * 128:(t + 1) * 128, :], [], [bxb])
                        bld.on("gpsimd", [betmp, bxb], [b_accg],
                               lambda e, st_=st_, xb=xb, etmp=etmp: e.tensor_tensor(
                                   out=accg[:, st_ * C:(st_ + 1) * C], in0=etmp[:, :],
                                   in1=xb[:, :], op=ALU.add))
                    else:
                        bld.on("gpsimd", [betmp, b_accg], [b_accg],
                               lambda e, st_=st_, etmp=etmp: e.tensor_tensor(
                                   out=accg[:, st_ * C:(st_ + 1) * C], in0=etmp[:, :],
                                   in1=accg[:, st_ * C:(st_ + 1) * C], op=ALU.add))

            for st_ in (range(NSUB) if l1sub >= 5 else []):
                t = t0 + st_
                bi = st_ % 2
                att_t = attg[:, st_ * C:(st_ + 1) * C]

                def atmm(e, att_t=att_t):
                    last = None
                    for ci in range(8):
                        last = e.transpose(out=ptr[0:128, ci * 128:(ci + 1) * 128],
                                           in_=att_t[:, ci * 128:(ci + 1) * 128],
                                           identity=ident_bf[:, :])
                    return last
                bld.on("tensor", [b_attg, b_ident_bf], [b_ptr], atmm)
                bld.on("vector", [b_ptr], [b_atfm],
                       lambda e: e.tensor_copy(out=atfm[:, :], in_=ptr[0:128, 0:1024]))

                for half, pob, bpob in ((0, prs1, b_prs1), (1, prs2, b_prs2)):
                    def wo1mm(e, half=half, pob=pob):
                        last = None
                        for ci in range(8):
                            last = e.matmul(
                                out=pob[0:128, 0:512],
                                lhsT=atfm[:, ci * 128:(ci + 1) * 128],
                                rhs=wslot[1][:, ci * C + half * 512:ci * C + half * 512 + 512],
                                start=(ci == 0), stop=(ci == 7))
                        return last
                    bld.on("tensor", [b_atfm, b_wslot[1]], [bpob], wo1mm)

                xo, bxo = x2b[bi], b_x2b[bi]

                def x2ev(e, st_=st_, xo=xo):
                    e.tensor_tensor(out=xo[:, 0:512], in0=prs1[:, 0:512],
                                    in1=accg[:, st_ * C:st_ * C + 512], op=ALU.add)
                    return e.tensor_tensor(out=xo[:, 512:1024], in0=prs2[:, 0:512],
                                           in1=accg[:, st_ * C + 512:(st_ + 1) * C], op=ALU.add)
                bld.on("vector", [b_prs1, b_prs2, b_accg], [bxo], x2ev)
                bld.dma(ch_st[bi], y_out[t * 128:(t + 1) * 128, :], xo[:, :],
                        [bxo], [Buf(f"y_t{t}")])

            if g + 1 < NG:
                load_wT(0, "W1q", chi=g)
                load_wT(1, "W1k", chi=g + 1)

        if l1sub < 5:
            bld.barrier()
            r = _finish_stub(nc, bld, es, y_out, xbuf, b_xbuf, ch_st, ch_x, x_in,
                             src_scr=x1_scr)
            es_c.close()
            return r
        for chn in ch_st:
            bld.wait_chan("gpsimd", chn)

        bld.alloc_sems(es)
        block = es.enter_context(nc.Block())
        bld.emit(block)
        es_c.close()
    return nc, dbg



def _consts():
    import ml_dtypes
    return {
        "ident_bf": np.eye(128, dtype=ml_dtypes.bfloat16),
        "ident_f32": np.eye(128, dtype=np.float32),
        "ones_bf": np.ones((128, 128), dtype=ml_dtypes.bfloat16),
        "ones_f32": np.ones((128, 128), dtype=np.float32),
        "iota8": np.tile(np.arange(8, dtype=np.float32), (128, 1)),
        "kvec": np.tile(np.array([1e-5, -2.5, 0.0, 0.0], dtype=np.float32), (128, 1)),
    }


def make_in_maps(**inputs):
    x = np.ascontiguousarray(inputs["L0_attn_x"], dtype=np.float32)
    consts = _consts()
    common = {
        "Wq": inputs["L0_attn_Wq"], "bq": inputs["L0_attn_bq"],
        "Wk1": inputs["L0_attn_Wk1"], "bk1": inputs["L0_attn_bk1"],
        "Wk2": inputs["L0_attn_Wk2"], "bk2": inputs["L0_attn_bk2"],
        "Wv1": inputs["L0_attn_Wv1"], "bv1": inputs["L0_attn_bv1"],
        "Wv2": inputs["L0_attn_Wv2"], "bv2": inputs["L0_attn_bv2"],
        "g1": inputs["L0_attn_g1"], "g2": inputs["L0_attn_g2"],
        "g3": inputs["L0_attn_g3"], "g4": inputs["L0_attn_g4"],
        "ln_g": inputs["L0_attn_ln_g"], "ln_b": inputs["L0_attn_ln_b"],
        "Wo0": inputs["L0_attn_Wo"], "bo0": inputs["L0_attn_bo"],
        "Wf": inputs["L0_ffn_W"], "bf": inputs["L0_ffn_b"],
        "lora_down": inputs["L0_ffn_lora_down"], "lora_up": inputs["L0_ffn_lora_up"],
        "down_aux": inputs["L0_ffn_down_aux"], "up_aux": inputs["L0_ffn_up_aux"],
        "W1q": inputs["L1_attn_Wq"], "W1k": inputs["L1_attn_Wk"],
        "W1v": inputs["L1_attn_Wv"], "W1o": inputs["L1_attn_Wo"],
        "Wg": inputs["L1_ffn_Wg"], "bg": inputs["L1_ffn_bg"],
        "We": inputs["L1_ffn_We"], "be": inputs["L1_ffn_be"],
    }
    common = {k: np.ascontiguousarray(np.asarray(v), dtype=np.float32)
              for k, v in common.items()}
    in_maps = []
    for c in range(NCORE):
        b, half = c >> 1, c & 1
        m = dict(common)
        m.update(consts)
        m["x"] = np.ascontiguousarray(x[b, half * TOK:(half + 1) * TOK, :])
        m["xp"] = np.ascontiguousarray(x[b, 0:PFX, :])
        in_maps.append(m)
    return in_maps


_cache = {}

def kernel(**inputs):
    if "nc" not in _cache:
        _cache["nc"], _ = build_program(debug=DEBUG_OUTPUTS, stage=STAGE, l1sub=L1SUB, attnparts=ATTNPARTS)
    nc = _cache["nc"]
    in_maps = make_in_maps(**inputs)
    res = run_bass_kernel_spmd(nc, in_maps, core_ids=list(range(NCORE)))
    out = np.empty((B, T, C), dtype=np.float32)
    for c in range(NCORE):
        b, half = c >> 1, c & 1
        out[b, half * TOK:(half + 1) * TOK, :] = res.results[c]["y"]
    return out
